# version 11 (final)
# baseline (speedup 1.0000x reference)
import numpy as np
import ml_dtypes
from contextlib import ExitStack

import concourse.bass as bass
import concourse.mybir as mybir
from concourse.bass_utils import run_bass_kernel_spmd

F32 = mybir.dt.float32
BF16 = mybir.dt.bfloat16
AF = mybir.ActivationFunctionType
ALU = mybir.AluOpType

NCORE = 8
SEG = 512
NSEG = 4
TOK = SEG * NSEG
D = 1024
KC = 8
H = 4096
EPS = 1e-5

VC_N1W, VC_N1B, VC_N2W, VC_N2B = 0, 8, 16, 24
VC_GATEB, VC_STATEB, VC_MLPB1, VC_INC = 32, 48, 56, 88
VC_BF, VC_BI, VC_BC = 96, 104, 112
VC_NCOL = 120





CHAIN_METHODS = frozenset((
    "tensor_copy", "tensor_tensor", "tensor_add", "tensor_sub", "tensor_mul",
    "tensor_max", "tensor_scalar", "tensor_single_scalar", "tensor_scalar_add",
    "tensor_scalar_sub", "tensor_scalar_mul", "tensor_scalar_max",
    "tensor_scalar_min", "tensor_relu", "tensor_reduce", "reduce_max",
    "reduce_sum", "scalar_tensor_tensor", "tensor_tensor_scan", "memset",
    "memzero", "reciprocal", "select", "copy_predicated", "transpose",
    "bn_stats", "bn_aggr", "activation", "copy", "add", "mul", "square",
    "sqrt", "sign",
))


class Chained:

    def __init__(self, eng, sem):
        self._eng = eng
        self._sem = sem
        self._n = 0
        self._nw = False

    def __getattr__(self, name):
        attr = getattr(self._eng, name)
        if name not in CHAIN_METHODS:
            return attr

        def w(*a, **k):
            ins = attr(*a, **k)
            self._n += 1
            ins.then_inc(self._sem, 1)
            if not self._nw:
                self._eng.wait_ge(self._sem, self._n)
            return ins
        return w


class _MockIns:
    def then_inc(self, *a, **k):
        return self

    def __getattr__(self, name):
        return lambda *a, **k: self


class MockChained:

    def __init__(self):
        self._n = 0
        self._nw = False

    def __getattr__(self, name):
        if name in CHAIN_METHODS:
            def w(*a, **k):
                self._n += 1
                return _MockIns()
            return w
        return lambda *a, **k: _MockIns()


class Emitter:
    ENGS = ("T", "V", "S", "G", "Y")
    CHAINED = ("V", "S", "G")
    DMA_SEMS = ("ld", "ld2", "hw", "st")
    DMA_OWNER = {"ld": "G", "ld2": "G", "hw": "Y", "st": "Y"}

    def __init__(self):
        self.q = {e: [] for e in self.ENGS}
        self.c = {k: 0 for k in ("T", "ld", "ld2", "hw", "st", "cc")}
        self.sems = {}
        self.maxwait = {}
        self.fenced = {}

    def raw(self, e, f):
        self.q[e].append(("plain", f))

    def wait(self, e, sem, val):
        if val is None:
            return
        if sem in self.DMA_SEMS:
            val = self.c[sem]
            if val <= 0:
                return
            self.maxwait[sem] = max(self.maxwait.get(sem, 0), val)
            self.raw(e, lambda eng, s=sem, v=val * 16: eng.wait_ge(self.sems[s], v))
        elif sem in self.CHAINED:
            assert isinstance(val, dict), (e, sem, val)
            self.raw(e, lambda eng, s=sem, m=val: eng.wait_ge(self.sems[s], m["v"]))
        else:
            if not isinstance(val, int) or val <= 0:
                return
            self.raw(e, lambda eng, s=sem, v=val: eng.wait_ge(self.sems[s], v))

    def step(self, e, f, inc=None, nw=True):
        if e in self.CHAINED and (inc is None or inc in self.CHAINED):
            import os as _os
            if _os.environ.get("K_NW_OFF", ""):
                off = _os.environ["K_NW_OFF"]
                if off == "all" or e in off:
                    nw = False
            m = {"v": None}
            self.q[e].append(("mark", f, m, nw))
            return m if inc is not None else None
        if inc is None:
            self.raw(e, f)
            return None
        if inc in self.DMA_SEMS:
            w = self.maxwait.get(inc, 0)
            if w > self.fenced.get(inc, 0):
                own = self.DMA_OWNER[inc]
                self.raw(own, lambda eng, s=inc, v=w * 16: eng.wait_ge(self.sems[s], v))
                self.fenced[inc] = w
        if inc in ("ld", "ld2"):
            for s2 in ("ld", "ld2"):
                if self.c[s2] > self.fenced.get("ring_" + s2, 0):
                    self.raw("G", lambda eng, s=s2, v=self.c[s2] * 16:
                             eng.wait_ge(self.sems[s], v))
                    self.fenced["ring_" + s2] = self.c[s2]
        if inc == "T" and self.c["T"] > 0:
            if getattr(f, "prefetch", False):
                prev = self.c["T"]
                inner = f

                def f(eng, inner=inner, prev=prev):
                    self._t_prev = prev
                    return inner(eng)
            else:
                self.raw(e, lambda eng, v=self.c["T"]: eng.wait_ge(self.sems["T"], v))
        self.c[inc] += 1
        mult = 16 if inc in self.DMA_SEMS else 1

        def g(eng, f=f, sem=inc, amt=mult):
            ins = f(eng)
            ins.then_inc(self.sems[sem], amt)
        self.raw(e, g)
        return self.c[inc]

    def tfence(self, eng):
        if getattr(self, "_t_prev", 0) > 0:
            eng.wait_ge(self.sems["T"], self._t_prev)
        self._t_prev = 0

    def mark_here(self, e):
        assert e in self.CHAINED
        m = {"v": None}
        self.q[e].append(("mark", None, m, False))
        return m

    def barrier(self, engines=("T", "V", "S", "G", "Y")):
        marks = {}
        for e in engines:
            if e == "Y":
                continue
            if e in self.CHAINED:
                marks[e] = ("chain", self.mark_here(e))
            else:
                self.c[e] += 1
                v = self.c[e]
                marks[e] = ("int", v)

                def f(eng, sem=e, v=v):
                    eng.nop().then_inc(self.sems[sem], 1)
                self.raw(e, f)
        for e in engines:
            for e2, (kind, v) in marks.items():
                if e2 == e:
                    continue
                self.wait(e, e2, v)

    def run_queue(self, e, eng):
        chained = isinstance(eng, (Chained, MockChained))
        for kind, *rest in self.q[e]:
            if kind == "plain":
                rest[0](eng)
            else:
                f, m, nw = rest
                if f is not None:
                    if chained:
                        eng._nw = nw
                    f(eng)
                    if chained:
                        eng._nw = False
                if chained:
                    m["v"] = eng._n

    def prepass(self):
        for e in self.CHAINED:
            self.run_queue(e, MockChained())


PHASE_ORDER = {"p0a": 0.1, "p0b": 0.2, "p0c": 0.3, "p0": 0.5, "a": 1, "exch": 2, "b": 3, "c": 4, "c2": 5, "d": 6}


def build_program(taps=(), stop_after="d", scratch=16384):
    PHASE_LVL = PHASE_ORDER[stop_after]
    nc = bass.Bass("TRN2", target_bir_lowering=False,
                   dynamic_dma_scratch_size=scratch)

    x_in = nc.dram_tensor("x", [TOK, D], F32, kind="ExternalInput").ap()
    gate_w = nc.dram_tensor("gate_w", [D, 2 * D], F32, kind="ExternalInput").ap()
    state_w = nc.dram_tensor("state_w", [D, D], F32, kind="ExternalInput").ap()
    out_w = nc.dram_tensor("out_w", [D, D], F32, kind="ExternalInput").ap()
    mlp_w1 = nc.dram_tensor("mlp_w1", [D, H], F32, kind="ExternalInput").ap()
    mlp_w2 = nc.dram_tensor("mlp_w2", [H, D], F32, kind="ExternalInput").ap()
    wmod_in = nc.dram_tensor("wmod", [D, 6 * D], F32, kind="ExternalInput").ap()
    cond_c = nc.dram_tensor("cond_c", [4, D], F32, kind="ExternalInput").ap()
    my_ffn_w = nc.dram_tensor("my_ffn_w", [D, D], F32, kind="ExternalInput").ap()
    vcols_in = nc.dram_tensor("vcols", [128, VC_NCOL], F32, kind="ExternalInput").ap()
    rows3 = nc.dram_tensor("rows3", [3, D], F32, kind="ExternalInput").ap()
    ident_in = nc.dram_tensor("ident", [128, 128], BF16, kind="ExternalInput").ap()
    mrow_in = nc.dram_tensor("mrow", [1, 8], F32, kind="ExternalInput").ap()
    invmrow_in = nc.dram_tensor("invmrow", [1, 8], F32, kind="ExternalInput").ap()
    mcols_in = nc.dram_tensor("mcols", [128, 8], F32, kind="ExternalInput").ap()

    out_hbm = nc.dram_tensor("out", [TOK, D], F32, kind="ExternalOutput").ap()
    tap_aps = {}
    for t in taps:
        shp, dt = {"xn": ([128, KC, SEG * NSEG], BF16),
                   "y": ([128, KC, TOK], BF16),
                   "xatt": ([128, 16, D], F32), "h2": ([128, KC, TOK], BF16),
                   "mod": ([4, 6 * D], BF16), "vin": ([1, NSEG, D], BF16),
                   "slin": ([128, NSEG, KC], F32), "xnw": ([128, 16, D], F32), "xn2": ([128, KC, SEG], BF16), "hid": ([128, 32, SEG], BF16), "w1q": ([128, KC, D], BF16)}[t]
        tap_aps[t] = nc.dram_tensor("tap_" + t, shp, dt, kind="ExternalOutput").ap()
    xn_tap_dram = (nc.dram_tensor("xn_tap_dram", [128, KC, TOK], BF16).ap()
                   if "xn" in taps else None)

    wT_bounce = nc.dram_tensor("wT_bounce", [128, KC * D], BF16)
    wT_all = nc.dram_tensor("wT_all", [NCORE * 128, KC * D], BF16)
    sl_bounce = nc.dram_tensor("sl_bounce", [128, 36], F32)
    sl_all = nc.dram_tensor("sl_all", [NCORE * 128, 36], F32)
    xnew_dram = nc.dram_tensor("xnew_dram", [TOK, D], F32).ap()

    E = Emitter()

    with ExitStack() as ctx:
        sb = lambda n, s, d: ctx.enter_context(nc.sbuf_tensor(n, s, d))
        ps = lambda n, s, d: ctx.enter_context(nc.psum_tensor(n, s, d))

        AW1 = sb("AW1", [128, KC, 2 * D], BF16)
        XNF = sb("XNF", [128, KC, 2 * D], BF16)
        YF = sb("YF", [128, KC, 2 * D], BF16)
        XATT = sb("XATT", [128, 16, D], F32)
        UF = sb("UF", [128, 3, SEG], BF16)
        FPA = sb("FPA", [128, 1, D], F32)
        FPB = sb("FPB", [128, 2, D], F32)
        ZTM = sb("ZTM", [128, D], BF16)
        BC = sb("BC", [128, 2, D], BF16)
        MOD = sb("MOD", [4, 6 * D], BF16)
        SCT = sb("SCT", [128, KC, 4], BF16)
        VC = sb("VC", [128, VC_NCOL], F32)
        MC = sb("MC", [128, 8], F32)
        BROW = sb("BROW", [1, D], BF16)
        IDT = sb("IDT", [128, 128], BF16)
        ONESB = sb("ONESB", [1, 128], BF16)
        ONESF = sb("ONESF", [1, 128], F32)
        ONESC = sb("ONESC", [128, 1], F32)
        MR = sb("MR", [1, 8], F32)
        IMR = sb("IMR", [1, 8], F32)
        CBSB = sb("CBSB", [128, 2, NSEG, KC], F32)
        SWS = sb("SWS", [128, KC], BF16)
        SWSF = sb("SWSF", [128, KC], F32)
        SBM = sb("SBM", [1, 1], F32)
        SUMA = sb("SUMA", [128, 16], F32)
        SQA = sb("SQA", [128, 16], F32)
        NMA = sb("NMA", [128, 16], F32)
        RSA = sb("RSA", [128, 16], F32)
        SUM2 = sb("SUM2", [128, 16], F32)
        SQ2 = sb("SQ2", [128, 16], F32)
        NM2 = sb("NM2", [128, 16], F32)
        RS2 = sb("RS2", [128, 16], F32)
        SUMB = sb("SUMB", [128, 16], F32)
        SQB = sb("SQB", [128, 16], F32)
        NMB = sb("NMB", [128, 16], F32)
        RSB = sb("RSB", [128, 16], F32)
        SUMH = sb("SUMH", [128, 2], F32)
        SQH = sb("SQH", [128, 2], F32)
        NM1 = sb("NM1", [128, 2], F32)
        RS1 = sb("RS1", [128, 2], F32)
        TMPC = sb("TMPC", [128, 16], F32)
        TMPD = sb("TMPD", [128, 16], F32)
        TMPE = sb("TMPE", [128, 16], F32)
        MMROWS = sb("MMROWS", [1, SEG], F32)
        DROW = sb("DROW", [1, SEG], F32)
        MODROW2 = MMROWS[0:1, :].bitcast(BF16)
        P_BF = sb("P_BF", [1, NSEG, SEG], BF16)
        SLST = sb("SLST", [128, NSEG, 9], F32)
        GATH = sb("GATH", [128, NCORE, 36], F32)
        SIN = sb("SIN", [128, NSEG, KC], F32)
        SINBF = sb("SINBF", [128, NSEG, KC], BF16)
        SLOCM = sb("SLOCM", [128, KC], F32)
        SINP = sb("SINP", [128, KC], F32)
        APR = sb("APR", [1, NSEG, 8], F32)
        VINR = XNF[0:1, 0:2, :].rearrange("p a (c d) -> p (a c) d", c=2)

        pA = [ps(f"pA{i}", [128, 512], F32) for i in range(4)]
        pB = [ps(f"pB{i}", [128, 512], F32) for i in range(2)]
        pC = ps("pC", [128, 512], F32)
        pT = ps("pT", [128, 1024], BF16)
        pT_F = pT[:, :].bitcast(F32)

        for name in ("T", "V", "S", "G", "ld", "ld2", "hw", "st", "cc"):
            E.sems[name] = ctx.enter_context(nc.semaphore("sem_" + name))

        XATT_BF_FLAT = XATT[:, :, :].bitcast(BF16).rearrange("p a b -> p (a b)")

        def Yap(s, k, j0, n):
            off = 8 * 2048 + s * 4096 + k * 512 + j0
            return XATT_BF_FLAT[:, off:off + n]

        SCBF = ZTM[0:4, :]
        ONESO = ONESF[:, :]
        state_wv = YF[:, :, 0:D]
        out_wv = YF[:, :, D:2 * D]
        HID = YF[:, :, :].rearrange("p a (b c) -> p (a b) c", b=4)
        wfT = AW1[:, :, 0:D]
        wiT = AW1[:, :, D:2 * D]
        wcT = XATT[:, 0:4, :].bitcast(BF16).rearrange("p a (b c) -> p (a b) c", b=2)
        ufT = XATT[:, 4:8, :].bitcast(BF16).rearrange("p a (b c) -> p (a b) c", b=2)
        uiT = YF[:, :, 0:D]
        ucT = YF[:, :, D:2 * D]
        woT = YF[:, :, 0:D]
        uoT = YF[:, :, D:2 * D]
        c_sb = XATT[:, 8:12, :].rearrange("p a (b c) -> p (a b) c", b=2)
        cbf = XATT[:, 12:14, :].bitcast(BF16).rearrange("p a (b c) -> p (a b) c", b=4)
        FIO = XATT[:, 14:16, :].rearrange("p a (b c) -> p (a b) c", b=2)
        zu_v = XATT[:, 8:12, :]
        XG = XATT[:, 0:2, :].bitcast(BF16).rearrange("p a (b c) -> p (a b) c", b=4)
        G2 = XATT[:, 2:4, :].bitcast(BF16).rearrange("p a (b c) -> p (a b) c", b=4)
        ZF = XATT[:, 4:6, :].bitcast(BF16).rearrange("p a (b c) -> p (a b) c", b=4)
        DA = XATT[:, 6, 0:512]
        DE = XATT[:, 6, 512:1024]
        TSTRIP = UF[:, 0:2, :].rearrange("p a b -> p (a b)")
        BMS = XATT[:, 7, 0:512]
        UST = XATT[:, 7, 512:1024].bitcast(BF16)[:, 0:512]
        CLA = FPB[0:4, 0, :]
        CLB = FPB[0:4, 1, :]
        CLC = FPA[0:4, 0, :]
        z2Fv = XNF[:, :, 0:512]
        xn2v = XNF[:, :, 512:1024]
        zbFv = AW1[:, :, 0:512]
        ffn_stage = AW1[:, :, 0:D]

        gd = lambda f: E.step("G", f, inc="ld")
        yd = lambda f: E.step("Y", f, inc="hw")

        ld_c = gd(lambda g: g.dma_start(out=SCBF[:, :], in_=cond_c[:, :]))
        ld_vc = yd(lambda y: y.dma_start(out=VC[:, :], in_=vcols_in[:, :]))
        ld_mc = yd(lambda y: y.dma_start(out=MC[:, :], in_=mcols_in[:, :]))
        ld_id = yd(lambda y: y.dma_start(out=IDT[:, :], in_=ident_in[:, :]))
        ld_mr = yd(lambda y: y.dma_start(out=MR[:, :], in_=mrow_in[:, :]))
        ld_imr = yd(lambda y: y.dma_start(out=IMR[:, :], in_=invmrow_in[:, :]))

        E.step("G", lambda g: g.memset(SLST[:, :, :], 0.0))
        E.step("G", lambda g: g.memset(ONESB[:, :], 1.0))
        E.step("G", lambda g: g.memset(ONESF[:, :], 1.0))
        E.step("G", lambda g: g.memset(ONESC[:, :], 1.0), inc="G")

        E.wait("S", "ld", ld_c)
        s_siluc = E.step("S", lambda s: s.activation(MOD[:, 0:D], SCBF[:, :], AF.Silu),
                         inc="S")
        E.wait("T", "S", s_siluc)
        E.wait("T", "hw", ld_id)

        def f_sct(t):
            ins = None
            for k in range(KC):
                ins = t.transpose(pT[:, k * 128:k * 128 + 4],
                                  MOD[0:4, k * 128:(k + 1) * 128], IDT[0:4, 0:4])
            return ins
        tv = E.step("T", f_sct, inc="T")
        E.wait("V", "T", tv)

        def f_sctev(v):
            ins = None
            for k in range(KC):
                ins = v.tensor_copy(SCT[:, k, :], pT[:, k * 128:k * 128 + 4])
            return ins
        v_sct = E.step("V", f_sctev, inc="V")
        E.wait("T", "V", v_sct)

        def colmath(SUM, SQ, NM, RS, c0, n, wv=()):
            for sem, val in wv:
                E.wait("V", sem, val)

            def f_v(v):
                v.tensor_scalar_mul(NM[:, c0:c0 + n], SUM[:, c0:c0 + n], -1.0 / D)
                v.tensor_mul(TMPC[:, c0:c0 + n], NM[:, c0:c0 + n], NM[:, c0:c0 + n])
                v.tensor_scalar(TMPD[:, c0:c0 + n], SQ[:, c0:c0 + n],
                                1.0 / D, float(EPS), ALU.mult, ALU.add)
                return v.tensor_sub(TMPE[:, c0:c0 + n], TMPD[:, c0:c0 + n],
                                    TMPC[:, c0:c0 + n])
            vm = E.step("V", f_v, inc="V", nw=False)
            E.wait("S", "V", vm)

            def f_s(s):
                s.activation(TMPC[:, c0:c0 + n], TMPE[:, c0:c0 + n], AF.Ln)
                return s.activation(RS[:, c0:c0 + n], TMPC[:, c0:c0 + n],
                                    AF.Exp, scale=-0.5)
            sm = E.step("S", f_s, inc="S", nw=False)
            return vm, sm

        def build_bcast(b, base, idx, plus1=False, wv=()):
            for e, sem, val in wv:
                E.wait(e, sem, val)

            def f_rex(t, b=b, base=base):
                t.matmul(pB[0][0:1, :], IDT[0:4, b:b + 1],
                         MOD[0:4, base: base + 512], start=True, stop=True)
                return t.matmul(pB[1][0:1, :], IDT[0:4, b:b + 1],
                                MOD[0:4, base + 512: base + 1024],
                                start=True, stop=True)
            tm = E.step("T", f_rex, inc="T")
            E.wait("S", "T", tm)
            sv = E.step("S", lambda s: (
                s.copy(MODROW2[0:1, 0:512], pB[0][0:1, :]),
                s.copy(MODROW2[0:1, 512:1024], pB[1][0:1, :]))[-1], inc="S")
            E.wait("T", "S", sv)
            vm = None
            for half in range(2):
                def f_o(t, half=half):
                    return t.matmul(pB[half][:, :], ONESB[:, :],
                                    MODROW2[0:1, half * 512:(half + 1) * 512],
                                    start=True, stop=True)
                tm = E.step("T", f_o, inc="T")
                E.wait("V", "T", tm)
                if plus1:
                    vm = E.step("V", lambda v, half=half, idx=idx: v.tensor_scalar_add(
                        BC[:, idx, half * 512:(half + 1) * 512], pB[half][:, :], 1.0),
                        inc="V")
                else:
                    vm = E.step("V", lambda v, half=half, idx=idx: v.tensor_copy(
                        BC[:, idx, half * 512:(half + 1) * 512], pB[half][:, :]),
                        inc="V")
                E.wait("T", "V", vm)
            return vm

        def build_CA_SA(b, wv=()):
            build_bcast(b, D, 0, plus1=True, wv=wv)
            return build_bcast(b, 0, 1)

        stat_sum = {}
        stat_sq = {}

        def emit_stats(seg):
            for t4 in range(4):
                t = seg * 4 + t4
                buf = FPA[:, 0, :] if t % 2 == 0 else FPB[:, 0, :]
                if t >= 2:
                    E.wait("Y", "V", stat_sum[t - 2])
                    E.wait("Y", "S", stat_sq[t - 2])
                hw_x = yd(lambda y, t=t, buf=buf: y.dma_start(
                    out=buf, in_=x_in[t * 128:(t + 1) * 128, :]))
                E.wait("V", "hw", hw_x)
                E.wait("S", "hw", hw_x)
                stat_sum[t] = E.step("V", lambda v, t=t, buf=buf: v.reduce_sum(
                    SUMA[:, t:t + 1], buf, axis=mybir.AxisListType.X), inc="V")
                stat_sq[t] = E.step("S", lambda sc, t=t, buf=buf: sc.activation(
                    ZTM[:, :], buf, AF.Square, accum_out=SQA[:, t:t + 1]), inc="S")
            colmath(SUMA, SQA, NMA, RSA, seg * 4, 4,
                    wv=[("S", stat_sq[seg * 4 + 3])])

        P2 = {"fh": None, "hwz": None}
        xn_mark = {}

        def emit_pass2(s, wv=(), zf_prev=None, xn_wait=None):
            build_CA_SA(s, wv=wv)
            if zf_prev is None:
                E.wait("V", "S", stat_sq[15])
                E.wait("Y", "V", stat_sum[14])
                E.wait("Y", "S", stat_sq[14])
            else:
                E.wait("Y", "S", xn_mark[zf_prev])
            for tc in range(4):
                cidx = 4 * s + tc
                if P2["fh"] is not None:
                    E.wait("Y", "V", P2["fh"])
                r0 = s * SEG + tc * 128
                hw0 = yd(lambda y, r0=r0: y.dma_start(
                    out=FPA[:, 0, :], in_=x_in[r0:r0 + 128, :]))
                E.wait("V", "hw", hw0)

                def f_h(v, cidx=cidx):
                    v.tensor_scalar(FPB[:, 1, :], FPA[:, 0, :],
                                    NMA[:, cidx:cidx + 1], RSA[:, cidx:cidx + 1],
                                    ALU.add, ALU.mult)
                    v.tensor_mul(FPB[:, 0, :], FPB[:, 1, :], BC[:, 0, :])
                    return v.scalar_tensor_tensor(
                        FPB[:, 1, :], FPB[:, 0, :], 0.0, BC[:, 1, :],
                        ALU.add, ALU.add, accum_out=SUMH[:, 0:1])
                vmh = E.step("V", f_h, inc="V")
                P2["fh"] = vmh
                E.wait("S", "V", vmh)
                if P2["hwz"] is not None:
                    E.wait("S", "hw", P2["hwz"])
                smh = E.step("S", lambda sc: sc.activation(
                    ZTM[:, :], FPB[:, 1, :], AF.Square,
                    accum_out=SQH[:, 0:1]), inc="S")
                _, sm_cm1 = colmath(SUMH, SQH, NM1, RS1, 0, 1, wv=[("S", smh)])
                E.wait("V", "S", sm_cm1)
                vmz = E.step("V", lambda v: v.tensor_scalar(
                    ZTM[:, :], FPB[:, 1, :], NM1[:, 0:1], RS1[:, 0:1],
                    ALU.add, ALU.mult), inc="V")
                E.wait("Y", "V", vmz)
                hw_z = yd(lambda y, tc=tc: y.dma_start(
                    out=ZF[:, :, tc * 128:(tc + 1) * 128], in_=ZTM[:, :],
                    transpose=True))
                P2["hwz"] = hw_z
                E.wait("V", "hw", hw_z)
            E.wait("S", "hw", P2["hwz"])
            if xn_wait is not None:
                E.wait("S", "T", xn_wait)

            def f_xn(sc, s=s):
                ins = None
                for k in range(KC):
                    ins = sc.activation(XNF[:, k, s * SEG:(s + 1) * SEG], ZF[:, k, :],
                                        AF.Identity,
                                        bias=VC[:, VC_N1B + k:VC_N1B + k + 1],
                                        scale=VC[:, VC_N1W + k:VC_N1W + k + 1])
                return ins
            xn_mark[s] = E.step("S", f_xn, inc="S")

        wmod_r = wmod_in.rearrange("(k p) o -> p k o", p=128)
        XNF01 = XNF[:, :, 0:D]
        mod_marks = []
        chunk_tmark = {}

        def emit_chunk(qi, wv=()):
            for e, sem, val in wv:
                E.wait(e, sem, val)
            buf = AW1[:, :, qi * D:(qi + 1) * D] if qi < 2 else XNF01
            if qi >= 3:
                E.wait("G", "T", chunk_tmark[qi - 1])
            ldq = gd(lambda g, qi=qi, buf=buf: g.dma_start(
                out=buf, in_=wmod_r[:, :, qi * D:(qi + 1) * D],
                max_dma_last_dim=8192))
            E.wait("T", "ld", ldq)
            for gi in range(2):
                def f_mod(t, gi=gi, buf=buf):
                    ins = None
                    for k in range(KC):
                        ins = t.matmul(pC[0:4, :], SCT[:, k, :],
                                       buf[:, k, gi * 512:(gi + 1) * 512],
                                       start=(k == 0), stop=(k == KC - 1))
                    return ins
                tvq = E.step("T", f_mod, inc="T")
                E.wait("S", "T", tvq)
                col0 = (qi * 2 + gi) * 512
                sv = E.step("S", lambda s, col0=col0: s.copy(
                    MOD[:, col0:col0 + 512], pC[0:4, :]), inc="S")
                mod_marks.append(sv)
                E.wait("T", "S", sv)
            chunk_tmark[qi] = E.step("T", lambda t: t.nop(), inc="T")

        emit_chunk(0)
        emit_stats(0)
        emit_stats(1)
        emit_chunk(1)
        emit_stats(2)
        emit_stats(3)
        E.wait("G", "T", chunk_tmark[1])
        ld_gate = gd(lambda g: g.dma_start(
            out=AW1[:, :, :], in_=gate_w.rearrange("(k p) o -> p k o", p=128),
            max_dma_last_dim=8192))
        ld_state = gd(lambda g: g.dma_start(
            out=state_wv, in_=state_w.rearrange("(k p) o -> p k o", p=128),
            max_dma_last_dim=8192))
        ld_outw = gd(lambda g: g.dma_start(
            out=out_wv, in_=out_w.rearrange("(k p) o -> p k o", p=128),
            max_dma_last_dim=8192))

        emit_pass2(2, wv=[("T", "S", mod_marks[3])])
        emit_pass2(3, zf_prev=2)

        E.wait("T", "hw", ld_vc)
        t_sbm = E.step("T", lambda t: t.matmul(
            pB[0][0:1, 0:8], ONESC[:, :], VC[:, VC_STATEB:VC_STATEB + 8],
            start=True, stop=True), inc="T")
        E.wait("V", "T", t_sbm)
        E.step("V", lambda v: v.reduce_sum(SBM[:, :], pB[0][0:1, 0:8],
                                           axis=mybir.AxisListType.X), nw=False)
        v_sbm = E.step("V", lambda v: v.tensor_scalar_mul(SBM[:, :], SBM[:, :],
                                                          1.0 / D), inc="V", nw=False)
        E.wait("T", "V", v_sbm)

        E.wait("V", "ld", ld_state)

        def f_sws(v):
            for k in range(KC):
                v.reduce_sum(SWSF[:, k:k + 1], state_wv[:, k, :],
                             axis=mybir.AxisListType.X)
            return v.tensor_copy(SWS[:, :], SWSF[:, :])
        v_sws = E.step("V", f_sws, inc="V", nw=False)

        evacG = {}
        xg_last = {}
        u_last = {}
        uS_last = {}
        uV_last = {}
        scan_last = {}
        de_mark = {}
        da_mark = {}
        fsl = {}

        def emit_amm(s, prev=None, pc_wait=None):
            if prev is None:
                E.wait("T", "ld", ld_gate)
            E.wait("T", "S", xn_mark[s])
            xg_prev = None
            for o in range(16):
                bank = pA[o % 4]
                pv = evacG.get((s, o - 4), evacG.get((prev, o + 12)))
                if pv is not None:
                    E.wait("T", "S", pv)

                def f_g(t, o=o, bank=bank, s=s):
                    ins = None
                    for k in range(KC):
                        if k == KC - 1:
                            E.tfence(t)
                        ins = t.matmul(bank[:, :], AW1[:, k, o * 128:(o + 1) * 128],
                                       XNF[:, k, s * SEG:(s + 1) * SEG],
                                       start=(k == 0), stop=(k == KC - 1))
                    return ins
                f_g.prefetch = True
                tm = E.step("T", f_g, inc="T")
                E.wait("S", "T", tm)
                if o < 8:
                    if xg_prev is not None:
                        E.wait("S", "V", xg_prev)
                    elif prev is not None:
                        E.wait("S", "V", xg_last[prev])
                    dst = UF[:, 2, :]
                else:
                    if o == 8 and prev is not None:
                        E.wait("S", "T", u_last[prev])
                    dst = G2[:, o - 8, :]
                sm = E.step("S", lambda sc, bank=bank, dst=dst, o=o: sc.activation(
                    dst, bank[:, :], AF.Silu,
                    bias=VC[:, VC_GATEB + o:VC_GATEB + o + 1]), inc="S")
                evacG[(s, o)] = sm
                if o < 8:
                    E.wait("V", "S", sm)
                    if o == 0 and prev is not None:
                        E.wait("V", "T", u_last[prev])
                    xg_prev = E.step("V", lambda v, o=o, s=s: v.tensor_mul(
                        XG[:, o, :], XNF[:, o, s * SEG:(s + 1) * SEG], UF[:, 2, :]),
                        inc="V")
            xg_last[s] = xg_prev

            if prev is None:
                E.wait("T", "V", v_sws)
            else:
                E.wait("T", "V", de_mark[prev])
                E.wait("T", "S", da_mark[prev])
            if pc_wait is not None:
                E.wait("T", "S", pc_wait)

            def f_mv(t, s=s):
                ins = None
                for k in range(KC):
                    ins = t.matmul(pC[0:1, :], SWS[:, k:k + 1],
                                   XNF[:, k, s * SEG:(s + 1) * SEG],
                                   start=(k == 0), stop=(k == KC - 1))
                return ins
            tm_mv = E.step("T", f_mv, inc="T")
            E.wait("S", "T", tm_mv)
            if prev is None:
                E.wait("S", "V", v_sbm)
            else:
                E.wait("S", "V", fsl[prev])

            def f_sp(sc):
                sc.activation(DROW[:, :], pC[0:1, :], AF.Exp, scale=1.0 / D,
                              bias=SBM[0:1, 0:1])
                return sc.activation(MMROWS[:, :], DROW[:, :], AF.Ln, bias=1.0)
            sm_sp = E.step("S", f_sp, inc="S", nw=False)
            E.wait("T", "S", sm_sp)
            E.wait("V", "S", sm_sp)
            vm_cs = E.step("V", lambda v: v.tensor_tensor_scan(
                DROW[:, :], MMROWS[:, :], MMROWS[:, :], 0.0,
                ALU.add, ALU.bypass), inc="V")
            E.wait("S", "V", vm_cs)
            sm_p = E.step("S", lambda sc, s=s: sc.activation(
                P_BF[:, s, :], DROW[:, :], AF.Exp, scale=-1.0), inc="S")

            def f_dade(t):
                return t.matmul(pC[:, :], ONESO, MMROWS[:, :], start=True,
                                stop=True)
            tm_dd = E.step("T", f_dade, inc="T")
            E.wait("V", "T", tm_dd)
            vm_de = E.step("V", lambda v: v.tensor_copy(DE, pC[:, :]), inc="V")
            E.wait("S", "V", vm_de)
            if prev is not None:
                E.wait("S", "V", scan_last[prev])
            sm_da = E.step("S", lambda sc: sc.activation(
                DA, pC[:, :], AF.Exp, scale=-1.0), inc="S")
            E.wait("T", "V", vm_de)
            E.wait("T", "S", sm_da)
            E.wait("G", "S", sm_da)
            de_mark[s] = vm_de
            da_mark[s] = sm_da

            E.wait("T", "V", xg_last[s])
            E.wait("T", "S", evacG[(s, 15)])
            if prev is None:
                E.wait("T", "ld", ld_state)
            gm = None
            vm_sc = None
            for o in range(KC):
                ub = o % 2

                def f_u(t, o=o):
                    ins = None
                    for k in range(KC):
                        t.matmul(pB[0][:, :], state_wv[:, k, o * 128:(o + 1) * 128],
                                 XG[:, k, :], start=(k == 0), stop=(k == KC - 1))
                        if k == KC - 1:
                            E.tfence(t)
                        ins = t.matmul(pB[1][:, :],
                                       state_wv[:, k, o * 128:(o + 1) * 128],
                                       G2[:, k, :], start=(k == 0),
                                       stop=(k == KC - 1))
                    return ins
                f_u.prefetch = True
                tm = E.step("T", f_u, inc="T")
                E.wait("S", "T", tm)
                sm_u = E.step("S", lambda sc, o=o: sc.activation(
                    UST[:, :], pB[0][:, :], AF.Identity,
                    bias=VC[:, VC_STATEB + o:VC_STATEB + o + 1]), inc="S")
                E.wait("V", "T", tm)
                vm_b = E.step("V", lambda v, o=o: v.scalar_tensor_tensor(
                    BMS[:, :], pB[1][:, :],
                    VC[:, VC_STATEB + o:VC_STATEB + o + 1],
                    DE, ALU.add, ALU.mult), inc="V")
                E.wait("G", "S", sm_u)
                E.wait("G", "V", vm_b)
                if o == 0 and prev is not None:
                    E.wait("G", "V", scan_last[prev])
                gm = E.step("G", lambda g, ub=ub: g.tensor_mul(
                    UF[:, ub, :], BMS[:, :], UST[:, :]), inc="G")
                E.wait("V", "G", gm)
                E.wait("T", "G", gm)
                vm_sc = E.step("V", lambda v, o=o, s=s, ub=ub: v.tensor_tensor_scan(
                    Yap(s, o, 0, SEG), DA, UF[:, ub, :], 0.0,
                    ALU.mult, ALU.add), inc="V")
                if o >= 1:
                    E.wait("G", "V", vm_sc)
                uS_last[s] = sm_u
                uV_last[s] = vm_b
                u_last[s] = tm
            scan_last[s] = vm_sc

            def f_sl(v, s=s):
                for k in range(KC):
                    v.tensor_copy(SLST[:, s, k:k + 1], Yap(s, k, 511, 1))
                return v.tensor_copy(SLST[0:1, s, 8:9], DROW[0:1, 511:512])
            fsl[s] = E.step("V", f_sl, inc="V")

        if PHASE_LVL >= 1:
            emit_amm(2)
            emit_chunk(2, wv=[("T", "V", de_mark[2]), ("T", "S", da_mark[2])])
            emit_chunk(3)
            emit_chunk(4)
            emit_chunk(5)
            emit_amm(3, prev=2, pc_wait=mod_marks[-1])
            emit_pass2(0, wv=[("T", "S", uS_last[3]), ("T", "V", uV_last[3])],
                       zf_prev=3, xn_wait=chunk_tmark[5])
            emit_amm(0, prev=3)
            emit_pass2(1, wv=[("T", "S", uS_last[0]), ("T", "V", uV_last[0])],
                       zf_prev=0)
            emit_amm(1, prev=0)

        if PHASE_LVL >= 2:
            E.barrier()
            if "y" in tap_aps:
                for s in range(NSEG):
                    for k in range(KC):
                        yd(lambda y, s=s, k=k: y.dma_start(
                            out=tap_aps["y"][:, k, s * SEG:(s + 1) * SEG],
                            in_=Yap(s, k, 0, SEG)))
            hw_sl = yd(lambda y: y.dma_start(
                out=sl_bounce[:, :], in_=SLST.rearrange("p a b -> p (a b)")))
            E.wait("G", "hw", hw_sl)
            cc_sl = E.step("G", lambda g: g.collective_compute(
                "AllGather", ALU.bypass, replica_groups=[list(range(NCORE))],
                ins=[sl_bounce.ap().opt()], outs=[sl_all.ap().opt()]), inc="cc")
            E.wait("G", "cc", cc_sl)
            ld_ga = gd(lambda g: g.dma_start(
                out=GATH[:, :, :], in_=sl_all.ap().rearrange("(j p) v -> p j v", p=128)))
            E.wait("S", "ld", ld_ga)
            E.wait("V", "ld", ld_ga)

            def f_apr(sc):
                ins = None
                for b in range(NSEG):
                    src = GATH[0:1, :, b * 9 + 8:b * 9 + 9].rearrange("p j v -> p (j v)")
                    ins = sc.activation(APR[:, b, :], src, AF.Exp, scale=-1.0)
                return ins
            sm_apr = E.step("S", f_apr, inc="S")
            E.wait("V", "S", sm_apr)
            E.wait("V", "hw", ld_imr)

            def f_aprm(v):
                ins = None
                for b in range(NSEG):
                    v.tensor_mul(SLOCM[0:1, :], APR[:, b, :], MR[:, :])
                    ins = v.tensor_add(APR[:, b, :], SLOCM[0:1, :], IMR[:, :])
                return ins
            vm_aprm = E.step("V", f_aprm, inc="V", nw=False)
            E.wait("T", "V", vm_aprm)

            def f_apc(t):
                ins = None
                for b in range(NSEG):
                    for j in range(NCORE):
                        ins = t.matmul(pC[:, b * 8 + j:b * 8 + j + 1], ONESO,
                                       APR[:, b, j:j + 1], start=True, stop=True)
                return ins
            tm_apc = E.step("T", f_apc, inc="T")
            E.wait("V", "T", tm_apc)

            def f_fold(v):
                ins = None
                for b in range(NSEG):
                    v.memset(SINP[:, :], 0.0)
                    cur, nxt = SINP[:, :], SIN[:, b, :]
                    for j in range(NCORE):
                        v.tensor_scalar_mul(SLOCM[:, :], GATH[:, j, b * 9:b * 9 + 8],
                                            MC[:, j:j + 1])
                        v.scalar_tensor_tensor(nxt, cur,
                                               pC[:, b * 8 + j:b * 8 + j + 1],
                                               SLOCM[:, :], ALU.mult, ALU.add)
                        cur, nxt = nxt, cur
                    v.tensor_copy(SIN[:, b, :], SINP[:, :])
                    ins = v.tensor_copy(SINBF[:, b, :], SIN[:, b, :])
                return ins
            vm_fold = E.step("V", f_fold, inc="V", nw=False)

            E.wait("T", "V", vm_fold)
            E.wait("T", "ld", ld_outw)
            for b in range(NSEG):
                for jg in range(2):
                    def f_vi(t, b=b, jg=jg):
                        ins = None
                        for k in range(KC):
                            ins = t.matmul(pB[0][0:1, :], SINBF[:, b, k:k + 1],
                                           out_wv[:, k, jg * 512:(jg + 1) * 512],
                                           start=(k == 0), stop=(k == KC - 1))
                        return ins
                    tm = E.step("T", f_vi, inc="T")
                    E.wait("S", "T", tm)
                    sm = E.step("S", lambda sc, b=b, jg=jg: sc.copy(
                        VINR[:, b, jg * 512:(jg + 1) * 512], pB[0][0:1, :]), inc="S")
                    E.wait("T", "S", sm)

            ld_brow = gd(lambda g: g.dma_start(out=BROW[:, :], in_=rows3[0:1, :]))
            E.barrier()

        if PHASE_LVL >= 3:
            import os as _os
            B_SEGS = int(_os.environ.get("K_B_SEGS", "4"))
            B_PARTS = int(_os.environ.get("K_B_PARTS", "7"))
            banks8 = pA + pB + [pC, pT_F]
            E.wait("T", "ld", ld_brow)
            for s in range(B_SEGS):
                if B_PARTS & 1:
                    vm_bc = build_CA_SA(s)
                for tc in range(4 if (B_PARTS & 2) else 0):
                    for jg in range(2):
                        bank = banks8[tc * 2 + jg]

                        def f_op(t, s=s, tc=tc, jg=jg, bank=bank):
                            ins = None
                            for k in range(KC):
                                ins = t.matmul(bank[:, :], Yap(s, k, tc * 128, 128),
                                               out_wv[:, k, jg * 512:(jg + 1) * 512],
                                               start=(k == 0), stop=False)
                            t.matmul(bank[:, :], ONESB[:, :],
                                     BROW[:, jg * 512:(jg + 1) * 512],
                                     start=False, stop=False)
                            E.tfence(t)
                            return t.matmul(bank[:, :],
                                            P_BF[:, s, tc * 128:(tc + 1) * 128],
                                            VINR[:, s, jg * 512:(jg + 1) * 512],
                                            start=False, stop=True)
                        f_op.prefetch = True
                        tm_op = E.step("T", f_op, inc="T")
                if B_PARTS & 2:
                    E.wait("V", "T", tm_op)

                for tc in range(4 if (B_PARTS & 4) else 0):
                    cidx = 4 * s + tc
                    r0 = s * SEG + tc * 128
                    hw0 = yd(lambda y, r0=r0: y.dma_start(
                        out=FPA[:, 0, :], in_=x_in[r0:r0 + 128, :]))
                    E.wait("V", "hw", hw0)

                    def f_hp(v, cidx=cidx):
                        v.tensor_scalar(FPB[:, 0, :], FPA[:, 0, :],
                                        NMA[:, cidx:cidx + 1], RSA[:, cidx:cidx + 1],
                                        ALU.add, ALU.mult)
                        v.tensor_mul(FPB[:, 1, :], FPB[:, 0, :], BC[:, 0, :])
                        return v.tensor_add(FPB[:, 0, :], FPB[:, 1, :], BC[:, 1, :])
                    vm_hp = E.step("V", f_hp, inc="V")
                    E.wait("Y", "V", vm_hp)

                    def f_xa(v, s=s, tc=tc):
                        for jg in range(2):
                            v.scalar_tensor_tensor(
                                XATT[:, 4 * s + tc, jg * 512:(jg + 1) * 512],
                                banks8[tc * 2 + jg][:, :], 0.0,
                                FPB[:, 0, jg * 512:(jg + 1) * 512],
                                ALU.add, ALU.add, accum_out=SUMH[:, jg:jg + 1])
                        return v.tensor_add(SUM2[:, 4 * s + tc:4 * s + tc + 1],
                                            SUMH[:, 0:1], SUMH[:, 1:2])
                    vm_xa = E.step("V", f_xa, inc="V", nw=False)
                    E.wait("S", "V", vm_xa)
                    sm = E.step("S", lambda sc, s=s, tc=tc: sc.activation(
                        ZTM[:, :], XATT[:, 4 * s + tc, :], AF.Square,
                        accum_out=SQ2[:, 4 * s + tc:4 * s + tc + 1]), inc="S")
                E.wait("T", "V", vm_xa)

            E.barrier()
            ld_brow2 = gd(lambda g: g.dma_start(out=BROW[:, :], in_=rows3[1:2, :]))
            if "xatt" in tap_aps:
                yd(lambda y: y.dma_start(out=tap_aps["xatt"], in_=XATT[:, :, :]))
            if B_PARTS & 4:
                _, sm_cm2 = colmath(SUM2, SQ2, NM2, RS2, 0, 16)
                E.wait("V", "S", sm_cm2)

            E.wait("G", "T", E.c["T"])
            ld_ffnw = gd(lambda g: g.dma_start(
                out=ffn_stage, in_=my_ffn_w.rearrange("(k p) o -> p k o", p=128),
                max_dma_last_dim=8192))
            E.wait("T", "ld", ld_ffnw)
            hw_wtb = None
            tw = None
            vcp = None
            for ki in range(KC):
                def f_wt(t, ki=ki):
                    ins = None
                    for km in range(KC):
                        ins = t.transpose(pT[:, km * 128:(km + 1) * 128],
                                          AW1[:, km, ki * 128:(ki + 1) * 128],
                                          IDT[:, :])
                    return ins
                if vcp is not None:
                    E.wait("T", "V", vcp)
                tw = E.step("T", f_wt, inc="T")
                E.wait("V", "T", tw)
                if hw_wtb is not None:
                    E.wait("V", "hw", hw_wtb)
                vcp = E.step("V", lambda v: v.tensor_copy(TSTRIP[:, :], pT[:, :]),
                             inc="V")
                E.wait("Y", "V", vcp)
                hw_wtb = yd(lambda y, ki=ki: y.dma_start(
                    out=wT_bounce.ap()[:, ki * D:(ki + 1) * D], in_=TSTRIP[:, :]))
            E.wait("T", "V", vcp)
            E.wait("G", "hw", hw_wtb)
            cc_wt = E.step("G", lambda g: g.collective_compute(
                "AllGather", ALU.bypass, replica_groups=[list(range(NCORE))],
                ins=[wT_bounce.ap().opt()], outs=[wT_all.ap().opt()]), inc="cc")

            clamp_vv = None
            for r0 in (2 * D, 5 * D):
                sva = E.step("S", lambda s, r0=r0: s.activation(
                    CLA, MOD[:, r0:r0 + D], AF.Abs), inc="S")
                svb = E.step("S", lambda s, r0=r0: s.activation(
                    CLB, MOD[:, r0:r0 + D], AF.Sign), inc="S")
                E.wait("V", "S", svb)
                E.step("V", lambda v: v.tensor_scalar_max(CLC, CLA, 0.25))
                vv = E.step("V", lambda v, r0=r0: v.tensor_mul(
                    MOD[:, r0:r0 + D], CLC, CLB), inc="V")
                E.wait("S", "V", vv)
                clamp_vv = vv
            E.wait("T", "V", clamp_vv)
            E.wait("Y", "V", clamp_vv)

            for vi, base in enumerate((4 * D, 3 * D)):
                for b in range(NSEG):
                    def f_rex2(t, base=base, b=b):
                        t.matmul(pB[0][0:1, :], IDT[0:4, b:b + 1],
                                 MOD[0:4, base: base + 512], start=True, stop=True)
                        return t.matmul(pB[1][0:1, :], IDT[0:4, b:b + 1],
                                        MOD[0:4, base + 512: base + 1024],
                                        start=True, stop=True)
                    tvq = E.step("T", f_rex2, inc="T")
                    E.wait("S", "T", tvq)
                    sv = E.step("S", lambda s: (
                        s.copy(MODROW2[0:1, 0:512], pB[0][0:1, :]),
                        s.copy(MODROW2[0:1, 512:1024], pB[1][0:1, :]))[-1], inc="S")
                    E.wait("T", "S", sv)

                    def f_ext(t):
                        ins = None
                        for k in range(KC):
                            ins = t.matmul(pC[:, k:k + 1],
                                           MODROW2[0:1, k * 128:(k + 1) * 128],
                                           ONESB[:, 0:1], start=True, stop=True)
                        return ins
                    tvq = E.step("T", f_ext, inc="T")
                    E.wait("V", "T", tvq)
                    if vi == 0:
                        vv = E.step("V", lambda v, b=b: v.tensor_scalar_add(
                            CBSB[:, 0, b, :], pC[:, 0:KC], 1.0), inc="V")
                    else:
                        vv = E.step("V", lambda v, b=b: v.tensor_copy(
                            CBSB[:, 1, b, :], pC[:, 0:KC]), inc="V")
                    E.wait("T", "V", vv)

        if PHASE_LVL >= 4:
            buf_lastT = {0: tw, 1: tw}
            last_xnew_v = {}
            q_bufs = (AW1[:, 0:4, :].rearrange("p a (b c) -> p (a b) c", b=2),
                      AW1[:, 4:8, :].rearrange("p a (b c) -> p (a b) c", b=2))
            for s in range(NSEG):
                vm_ga = build_bcast(s, 2 * D, 0)

                sq_last = None
                for tc in range(4):
                    vm_z2 = E.step("V", lambda v, s=s, tc=tc: v.tensor_scalar(
                        ZTM[:, :], XATT[:, 4 * s + tc, :],
                        NM2[:, 4 * s + tc:4 * s + tc + 1],
                        RS2[:, 4 * s + tc:4 * s + tc + 1], ALU.add, ALU.mult), inc="V")
                    E.wait("Y", "V", vm_z2)
                    hw_z = yd(lambda y, tc=tc: y.dma_start(
                        out=z2Fv[:, :, tc * 128:(tc + 1) * 128], in_=ZTM[:, :],
                        transpose=True))
                    E.wait("V", "hw", hw_z)
                E.wait("S", "hw", E.c["hw"])

                def f_xn2(sc):
                    ins = None
                    for k in range(KC):
                        ins = sc.activation(xn2v[:, k, :], z2Fv[:, k, :], AF.Identity,
                                            bias=VC[:, VC_N2B + k:VC_N2B + k + 1],
                                            scale=VC[:, VC_N2W + k:VC_N2W + k + 1])
                    return ins
                sm_xn2 = E.step("S", f_xn2, inc="S")
                if s == 0 and "xn2" in tap_aps:
                    E.wait("Y", "S", sm_xn2)
                    yd(lambda y: y.dma_start(out=tap_aps["xn2"], in_=xn2v[:, :, :]))
                E.wait("T", "S", sm_xn2)

                evm = {}
                for q in range(4):
                    buf = q_bufs[q % 2]
                    if buf_lastT.get(q % 2):
                        E.wait("G", "T", buf_lastT[q % 2])
                    ldq = gd(lambda g, q=q, buf=buf: g.dma_start(
                        out=buf,
                        in_=mlp_w1.rearrange("(k p) o -> p k o", p=128)[:, :, q * 1024:(q + 1) * 1024],
                        max_dma_last_dim=8192))
                    E.wait("T", "ld", ldq)
                    if s == 0 and q == 0 and "w1q" in tap_aps:
                        E.wait("Y", "ld", ldq)
                        yd(lambda y, buf=buf: y.dma_start(out=tap_aps["w1q"], in_=buf))
                    for hc in range(8):
                        hidx = q * 8 + hc
                        bank = pA[hidx % 4]
                        if hidx >= 4:
                            E.wait("T", "S", evm[hidx - 4])

                        def f_m1(t, hc=hc, bank=bank, buf=buf):
                            ins = None
                            for k in range(KC):
                                if k == KC - 1:
                                    E.tfence(t)
                                ins = t.matmul(bank[:, :], buf[:, k, hc * 128:(hc + 1) * 128],
                                               xn2v[:, k, :], start=(k == 0),
                                               stop=(k == KC - 1))
                            return ins
                        f_m1.prefetch = True
                        tm = E.step("T", f_m1, inc="T")
                        buf_lastT[q % 2] = tm
                        E.wait("S", "T", tm)
                        sm = E.step("S", lambda sc, hidx=hidx, bank=bank: sc.activation(
                            HID[:, hidx, :], bank[:, :], AF.Gelu,
                            bias=VC[:, VC_MLPB1 + hidx:VC_MLPB1 + hidx + 1]), inc="S")
                        evm[hidx] = sm
                E.wait("T", "S", evm[31])
                if s == 0 and "hid" in tap_aps:
                    E.wait("Y", "S", evm[31])
                    yd(lambda y: y.dma_start(out=tap_aps["hid"], in_=HID[:, :, :]))

                for q in range(4):
                    buf = q_bufs[q % 2]
                    if buf_lastT.get(q % 2):
                        E.wait("G", "T", buf_lastT[q % 2])
                    ldq = gd(lambda g, q=q, buf=buf: g.dma_start(
                        out=buf,
                        in_=mlp_w2.rearrange("(k p) o -> p k o", p=128)[:, q * 8:(q + 1) * 8, :],
                        max_dma_last_dim=8192))
                    E.wait("T", "ld", ldq)

                    def f_m2(t, q=q, buf=buf):
                        ins = None
                        for tc in range(4):
                            for jg in range(2):
                                bank = banks8[tc * 2 + jg]
                                for k8 in range(8):
                                    if tc == 3 and jg == 1 and k8 == 7:
                                        E.tfence(t)
                                    ins = t.matmul(bank[:, :],
                                                   HID[:, q * 8 + k8, tc * 128:(tc + 1) * 128],
                                                   buf[:, k8, jg * 512:(jg + 1) * 512],
                                                   start=(q == 0 and k8 == 0), stop=False)
                        return ins
                    f_m2.prefetch = True
                    tmq = E.step("T", f_m2, inc="T")
                    buf_lastT[q % 2] = tmq

                E.wait("T", "ld", ld_brow2)

                def f_m2b(t):
                    ins = None
                    for tc in range(4):
                        for jg in range(2):
                            if tc == 3 and jg == 1:
                                E.tfence(t)
                            ins = t.matmul(banks8[tc * 2 + jg][:, :], ONESB[:, :],
                                           BROW[:, jg * 512:(jg + 1) * 512],
                                           start=False, stop=True)
                    return ins
                f_m2b.prefetch = True
                tm_m2 = E.step("T", f_m2b, inc="T")
                E.wait("V", "T", tm_m2)

                for tc in range(4):
                    r0 = s * SEG + tc * 128
                    if last_xnew_v.get(0):
                        E.wait("Y", "V", last_xnew_v[0])
                    hw_x = yd(lambda y, r0=r0: y.dma_start(
                        out=FPA[:, 0, :], in_=x_in[r0:r0 + 128, :]))

                    def f_a2(v, tc=tc, s=s):
                        for jg in range(2):
                            v.tensor_add(FPB[:, 0, jg * 512:(jg + 1) * 512],
                                         banks8[tc * 2 + jg][:, :],
                                         XATT[:, 4 * s + tc, jg * 512:(jg + 1) * 512])
                        return v.engine_nop()
                    vm_a2 = E.step("V", f_a2, inc="V")
                    gm = E.step("V", lambda v: v.tensor_mul(
                        FPB[:, 1, :], FPB[:, 0, :], BC[:, 0, :]), inc="V")
                    E.wait("V", "hw", hw_x)
                    vm_xn = E.step("V", lambda v, s=s, tc=tc: v.scalar_tensor_tensor(
                        XATT[:, 4 * s + tc, :], FPB[:, 1, :], 0.0, FPA[:, 0, :],
                        ALU.add, ALU.add,
                        accum_out=SUMB[:, 4 * s + tc:4 * s + tc + 1]), inc="V")
                    last_xnew_v[0] = vm_xn
                    E.wait("S", "V", vm_xn)
                    sm = E.step("S", lambda sc, s=s, tc=tc: sc.activation(
                        ZTM[:, :], XATT[:, 4 * s + tc, :], AF.Square,
                        accum_out=SQB[:, 4 * s + tc:4 * s + tc + 1]), inc="S")
                    E.wait("T", "V", vm_xn)

            E.barrier()
            _, sm_cmb = colmath(SUMB, SQB, NMB, RSB, 0, 16)
            E.wait("V", "S", sm_cmb)

        if PHASE_LVL >= 5:
            hw_bounce = None
            sm_h2 = None
            for s in range(NSEG):
                if sm_h2 is not None:
                    E.wait("Y", "S", sm_h2)
                for tc in range(4):
                    vm_zb = E.step("V", lambda v, s=s, tc=tc: v.tensor_scalar(
                        ZTM[:, :], XATT[:, 4 * s + tc, :],
                        NMB[:, 4 * s + tc:4 * s + tc + 1],
                        RSB[:, 4 * s + tc:4 * s + tc + 1], ALU.add, ALU.mult), inc="V")
                    E.wait("Y", "V", vm_zb)
                    hw_z = yd(lambda y, tc=tc: y.dma_start(
                        out=zbFv[:, :, tc * 128:(tc + 1) * 128], in_=ZTM[:, :],
                        transpose=True))
                    E.wait("V", "hw", hw_z)
                E.wait("S", "hw", E.c["hw"])

                def f_h2(sc, s=s):
                    ins = None
                    for k in range(KC):
                        ins = sc.activation(XNF[:, k, s * SEG:(s + 1) * SEG],
                                            zbFv[:, k, :], AF.Identity,
                                            bias=CBSB[:, 1, s, k:k + 1],
                                            scale=CBSB[:, 0, s, k:k + 1])
                    return ins
                sm_h2 = E.step("S", f_h2, inc="S")
                for tc in range(4):
                    r0 = s * SEG + tc * 128
                    hw_bounce = yd(lambda y, r0=r0, s=s, tc=tc: y.dma_start(
                        out=xnew_dram[r0:r0 + 128, :], in_=XATT[:, 4 * s + tc, :]))

            E.barrier()
            E.wait("Y", "hw", hw_bounce)
            E.wait("G", "hw", hw_bounce)

        if "h2" in tap_aps:
            def f_taph(y):
                return y.dma_start(out=tap_aps["h2"], in_=XNF[:, :, :])
            E.step("Y", f_taph, inc="hw")
        if "mod" in tap_aps:
            def f_tapm(y):
                return y.dma_start(out=tap_aps["mod"], in_=MOD[:, :])
            E.step("Y", f_tapm, inc="hw")
        if "vin" in tap_aps:
            yd(lambda y: y.dma_start(out=tap_aps["vin"], in_=VINR[0:1, :, :]))
        if "slin" in tap_aps:
            yd(lambda y: y.dma_start(out=tap_aps["slin"], in_=SIN[:, :, :]))
        if "xn" in tap_aps:
            yd(lambda y: y.dma_start(out=tap_aps["xn"], in_=xn_tap_dram))
        if taps:
            E.barrier()

        if PHASE_LVL < 6:
            E.barrier()
            st = E.step("Y", lambda y: y.dma_start(
                out=out_hbm[0:128, :], in_=FPA[:, 0, :]), inc="st")
        if PHASE_LVL >= 6:
            ld_brow3 = gd(lambda g: g.dma_start(out=BROW[:, :], in_=rows3[2:3, :]))
            E.wait("Y", "cc", cc_wt)
            d_slots = [("wf", wfT), ("uf", ufT), ("wi", wiT), ("ui", uiT),
                       ("wc", wcT), ("uc", ucT)]
            slot_idx = {"wf": 0, "uf": 1, "wi": 2, "ui": 3, "wc": 4, "uc": 5,
                        "wo": 6, "uo": 7}
            hw_gate_w = None
            for name, dst in d_slots:
                j = slot_idx[name]
                hw_gate_w = yd(lambda y, j=j, dst=dst: y.dma_start(
                    out=dst,
                    in_=wT_all.ap()[j * 128:(j + 1) * 128, :].rearrange(
                        "p (a b) -> p a b", a=KC)))

            def f_cinit(v):
                ins = None
                for k in range(KC):
                    ins = v.tensor_scalar(c_sb[:, k, :], XNF[:, k, 0:512],
                                          0.0, VC[:, VC_INC + k:VC_INC + k + 1],
                                          ALU.mult, ALU.add)
                return ins
            E.wait("V", "hw", 1)
            vm_c0 = E.step("V", f_cinit, inc="V")
            E.wait("G", "V", vm_c0)

            E.wait("T", "hw", hw_gate_w)
            gate_defs = [(wfT, ufT, VC_BF, AF.Sigmoid, 0),
                         (wiT, uiT, VC_BI, AF.Sigmoid, 1),
                         (wcT, ucT, VC_BC, AF.Tanh, 2)]
            gm = None
            for t_step in range(NSEG):
                def f_cc(sc):
                    ins = None
                    for k in range(KC):
                        ins = sc.copy(cbf[:, k, :], c_sb[:, k, :])
                    return ins
                if t_step > 0:
                    E.wait("S", "G", gm)
                else:
                    E.wait("S", "V", vm_c0)
                gm_cc = E.step("S", f_cc, inc="S")
                E.wait("T", "S", gm_cc)

                prev_evac = {}
                for o in range(KC):
                    for wT_, uT_, vb, af, slot in gate_defs:
                        bank = pA[slot]

                        def f_gate(t, wT_=wT_, uT_=uT_, o=o, bank=bank, t_step=t_step):
                            ins = None
                            for k in range(KC):
                                ins = t.matmul(bank[:, :], wT_[:, k, o * 128:(o + 1) * 128],
                                               XNF[:, k, t_step * SEG:(t_step + 1) * SEG],
                                               start=(k == 0), stop=False)
                            for k in range(KC):
                                if k == KC - 1:
                                    E.tfence(t)
                                ins = t.matmul(bank[:, :], uT_[:, k, o * 128:(o + 1) * 128],
                                               cbf[:, k, :], start=False,
                                               stop=(k == KC - 1))
                            return ins
                        f_gate.prefetch = True
                        if o >= 1 and prev_evac.get(slot):
                            E.wait("T", "S", prev_evac[slot])
                        tm = E.step("T", f_gate, inc="T")
                        E.wait("S", "T", tm)
                        if o >= 1 and gm is not None:
                            E.wait("S", "G", gm)
                        sm = E.step("S", lambda sc, bank=bank, vb=vb, o=o, af=af,
                                    slot=slot: sc.activation(
                            FIO[:, slot, :], bank[:, :], af,
                            bias=VC[:, vb + o:vb + o + 1]), inc="S")
                        prev_evac[slot] = sm
                    E.wait("G", "S", sm)

                    def f_cu(g, o=o):
                        g.tensor_mul(FIO[:, 3, :], FIO[:, 0, :], c_sb[:, o, :])
                        g.tensor_mul(FIO[:, 0, :], FIO[:, 1, :], FIO[:, 2, :])
                        return g.tensor_add(c_sb[:, o, :], FIO[:, 3, :], FIO[:, 0, :])
                    gm = E.step("G", f_cu, inc="G")

            def f_ccf(sc):
                ins = None
                for k in range(KC):
                    ins = sc.copy(cbf[:, k, :], c_sb[:, k, :])
                return ins
            E.wait("S", "G", gm)
            gm_ccf = E.step("S", f_ccf, inc="S")

            t_gates_done = E.step("T", lambda t: t.nop(), inc="T")
            E.wait("Y", "T", t_gates_done)
            yd(lambda y: y.dma_start(
                out=woT, in_=wT_all.ap()[6 * 128:7 * 128, :].rearrange(
                    "p (a b) -> p a b", a=KC)))
            hw_wo = yd(lambda y: y.dma_start(
                out=uoT, in_=wT_all.ap()[7 * 128:8 * 128, :].rearrange(
                    "p (a b) -> p a b", a=KC)))

            E.wait("T", "S", gm_ccf)
            E.wait("T", "hw", hw_wo)
            E.wait("T", "ld", ld_brow3)
            for tc in range(4):
                for jg in range(2):
                    bank = pB[jg]

                    def f_zu(t, tc=tc, jg=jg, bank=bank):
                        ins = None
                        for k in range(KC):
                            ins = t.matmul(bank[:, :], cbf[:, k, tc * 128:(tc + 1) * 128],
                                           uoT[:, k, jg * 512:(jg + 1) * 512],
                                           start=(k == 0), stop=False)
                        E.tfence(t)
                        return t.matmul(bank[:, :], ONESB[:, :],
                                        BROW[:, jg * 512:(jg + 1) * 512],
                                        start=False, stop=True)
                    f_zu.prefetch = True
                    tm = E.step("T", f_zu, inc="T")
                    E.wait("V", "T", tm)
                    vm = E.step("V", lambda v, tc=tc, jg=jg, bank=bank: v.tensor_copy(
                        zu_v[:, tc, jg * 512:(jg + 1) * 512], bank[:, :]), inc="V")
                    E.wait("T", "V", vm)

            last_vo = None
            last_st = None
            for t_step in range(NSEG):
                vm_gb = build_bcast(t_step, 5 * D, 1)
                for tc in range(4):
                    r0 = t_step * SEG + tc * 128
                    if last_vo:
                        E.wait("Y", "V", last_vo)
                    hw_xn = yd(lambda y, r0=r0: y.dma_start(
                        out=FPA[:, 0, :], in_=xnew_dram[r0:r0 + 128, :]))
                    for jg in range(2):
                        bank = pA[jg]

                        def f_wo(t, t_step=t_step, tc=tc, jg=jg, bank=bank):
                            ins = None
                            for k in range(KC):
                                if k == KC - 1:
                                    E.tfence(t)
                                ins = t.matmul(bank[:, :],
                                               XNF[:, k, t_step * SEG + tc * 128:
                                                   t_step * SEG + (tc + 1) * 128],
                                               woT[:, k, jg * 512:(jg + 1) * 512],
                                               start=(k == 0), stop=(k == KC - 1))
                            return ins
                        f_wo.prefetch = True
                        tm = E.step("T", f_wo, inc="T")
                    E.wait("V", "T", tm)

                    def f_s1(v, tc=tc):
                        for jg in range(2):
                            v.tensor_add(FPB[:, 0, jg * 512:(jg + 1) * 512],
                                         pA[jg][:, :],
                                         zu_v[:, tc, jg * 512:(jg + 1) * 512])
                        return v.engine_nop()
                    vm_s1 = E.step("V", f_s1, inc="V")
                    E.wait("T", "V", vm_s1)
                    E.wait("S", "V", vm_s1)
                    if last_st:
                        E.wait("S", "st", last_st)
                    sm_th = E.step("S", lambda sc: sc.activation(
                        FPB[:, 1, :], FPB[:, 0, :], AF.Tanh), inc="S")
                    E.wait("V", "S", sm_th)
                    E.wait("V", "hw", hw_xn)
                    if last_st:
                        E.wait("V", "st", last_st)

                    def f_out(v):
                        v.tensor_mul(FPB[:, 0, :], FPB[:, 1, :], BC[:, 1, :])
                        return v.tensor_add(FPB[:, 1, :], FPB[:, 0, :],
                                            FPA[:, 0, :])
                    vm_o = E.step("V", f_out, inc="V")
                    last_vo = vm_o
                    E.wait("Y", "V", vm_o)
                    st = E.step("Y", lambda y, r0=r0: y.dma_start(
                        out=out_hbm[r0:r0 + 128, :], in_=FPB[:, 1, :]), inc="st")
                    last_st = st

        E.wait("Y", "st", E.c["st"])
        E.wait("G", "st", E.c["st"])

        chain_sems = {e: ctx.enter_context(nc.semaphore("chain_" + e))
                      for e in ("V", "S", "G")}
        E.sems["V"] = chain_sems["V"]
        E.sems["S"] = chain_sems["S"]
        E.sems["G"] = chain_sems["G"]
        E.prepass()
        with nc.Block() as block:
            @block.tensor
            def _(eng):
                E.run_queue("T", eng)

            @block.vector
            def _(eng):
                E.run_queue("V", Chained(eng, chain_sems["V"]))

            @block.scalar
            def _(eng):
                E.run_queue("S", Chained(eng, chain_sems["S"]))

            @block.gpsimd
            def _(eng):
                E.run_queue("G", Chained(eng, chain_sems["G"]))

            @block.sync
            def _(eng):
                E.run_queue("Y", eng)

    return nc


def _cols(vec, n):
    return np.ascontiguousarray(
        np.asarray(vec, dtype=np.float32).reshape(n, 128).T)


def prepare_inputs(inputs):
    x = np.asarray(inputs["L0_attn_x"], dtype=np.float32)
    vcols = np.concatenate([
        _cols(inputs["L0_attn_norm1_w"], 8),
        _cols(inputs["L0_attn_norm1_b"], 8),
        _cols(inputs["L0_attn_norm2_w"], 8),
        _cols(inputs["L0_attn_norm2_b"], 8),
        _cols(inputs["L0_attn_gate_b"], 16),
        _cols(inputs["L0_attn_state_b"], 8),
        _cols(inputs["L0_attn_mlp_b1"], 32),
        _cols(inputs["L0_ffn_in_c"], 8),
        _cols(inputs["L0_ffn_bf"], 8),
        _cols(inputs["L0_ffn_bi"], 8),
        _cols(inputs["L0_ffn_bc"], 8),
    ], axis=1)
    rows3 = np.stack([
        np.asarray(inputs["L0_attn_out_b"], dtype=np.float32),
        np.asarray(inputs["L0_attn_mlp_b2"], dtype=np.float32),
        np.asarray(inputs["L0_ffn_bo"], dtype=np.float32),
    ], axis=0)
    ident = np.eye(128, dtype=ml_dtypes.bfloat16)
    ffn_ws = [np.ascontiguousarray(np.asarray(inputs[k], dtype=np.float32))
              for k in ("L0_ffn_wf", "L0_ffn_uf", "L0_ffn_wi", "L0_ffn_ui",
                        "L0_ffn_wc", "L0_ffn_uc", "L0_ffn_wo", "L0_ffn_uo")]
    shared = dict(
        gate_w=np.ascontiguousarray(np.asarray(inputs["L0_attn_gate_w"], dtype=np.float32)),
        state_w=np.ascontiguousarray(np.asarray(inputs["L0_attn_state_w"], dtype=np.float32)),
        out_w=np.ascontiguousarray(np.asarray(inputs["L0_attn_out_w"], dtype=np.float32)),
        mlp_w1=np.ascontiguousarray(np.asarray(inputs["L0_attn_mlp_w1"], dtype=np.float32)),
        mlp_w2=np.ascontiguousarray(np.asarray(inputs["L0_attn_mlp_w2"], dtype=np.float32)),
        wmod=np.ascontiguousarray(np.asarray(inputs["L0_wmod"], dtype=np.float32)),
        cond_c=np.ascontiguousarray(np.asarray(inputs["L0_c"], dtype=np.float32)),
        vcols=vcols, rows3=rows3, ident=ident,
    )
    in_maps = []
    for c in range(NCORE):
        m = dict(shared)
        m["x"] = np.ascontiguousarray(
            x[:, SEG * c:SEG * (c + 1), :].reshape(TOK, D))
        m["my_ffn_w"] = ffn_ws[c]
        mask = (np.arange(8) < c).astype(np.float32)
        m["mrow"] = mask.reshape(1, 8).copy()
        m["invmrow"] = (1.0 - mask).reshape(1, 8).copy()
        m["mcols"] = np.ascontiguousarray(
            np.broadcast_to(mask, (128, 8)).astype(np.float32))
        in_maps.append(m)
    return in_maps


def kernel(**inputs):
    in_maps = prepare_inputs(inputs)
    nc = build_program()
    res = run_bass_kernel_spmd(nc, in_maps, core_ids=list(range(NCORE)))
    out = np.empty((NSEG, NCORE * SEG, D), dtype=np.float32)
    for c in range(NCORE):
        r = np.asarray(res.results[c]["out"]).reshape(NSEG, SEG, D)
        out[:, SEG * c:SEG * (c + 1), :] = r
    return out


# version 12
# speedup vs baseline: 1.0172x; 1.0172x over previous
import numpy as np
import ml_dtypes
from contextlib import ExitStack

import concourse.bass as bass
import concourse.mybir as mybir
from concourse.bass_utils import run_bass_kernel_spmd

F32 = mybir.dt.float32
BF16 = mybir.dt.bfloat16
AF = mybir.ActivationFunctionType
ALU = mybir.AluOpType

NCORE = 8
SEG = 512
NSEG = 4
TOK = SEG * NSEG
D = 1024
KC = 8
H = 4096
EPS = 1e-5

VC_N1W, VC_N1B, VC_N2W, VC_N2B = 0, 8, 16, 24
VC_GATEB, VC_STATEB, VC_MLPB1, VC_INC = 32, 48, 56, 88
VC_BF, VC_BI, VC_BC = 96, 104, 112
VC_NCOL = 120





CHAIN_METHODS = frozenset((
    "tensor_copy", "tensor_tensor", "tensor_add", "tensor_sub", "tensor_mul",
    "tensor_max", "tensor_scalar", "tensor_single_scalar", "tensor_scalar_add",
    "tensor_scalar_sub", "tensor_scalar_mul", "tensor_scalar_max",
    "tensor_scalar_min", "tensor_relu", "tensor_reduce", "reduce_max",
    "reduce_sum", "scalar_tensor_tensor", "tensor_tensor_scan", "memset",
    "memzero", "reciprocal", "select", "copy_predicated", "transpose",
    "bn_stats", "bn_aggr", "activation", "copy", "add", "mul", "square",
    "sqrt", "sign",
))


class Chained:

    def __init__(self, eng, sem):
        self._eng = eng
        self._sem = sem
        self._n = 0
        self._nw = False

    def __getattr__(self, name):
        attr = getattr(self._eng, name)
        if name not in CHAIN_METHODS:
            return attr

        def w(*a, **k):
            ins = attr(*a, **k)
            self._n += 1
            ins.then_inc(self._sem, 1)
            if not self._nw:
                self._eng.wait_ge(self._sem, self._n)
            return ins
        return w


class _MockIns:
    def then_inc(self, *a, **k):
        return self

    def __getattr__(self, name):
        return lambda *a, **k: self


class MockChained:

    def __init__(self):
        self._n = 0
        self._nw = False

    def __getattr__(self, name):
        if name in CHAIN_METHODS:
            def w(*a, **k):
                self._n += 1
                return _MockIns()
            return w
        return lambda *a, **k: _MockIns()


class Emitter:
    ENGS = ("T", "V", "S", "G", "Y")
    CHAINED = ("V", "S", "G")
    DMA_SEMS = ("ld", "ld2", "hw", "st")
    DMA_OWNER = {"ld": "G", "ld2": "G", "hw": "Y", "st": "Y"}

    def __init__(self):
        self.q = {e: [] for e in self.ENGS}
        self.c = {k: 0 for k in ("T", "ld", "ld2", "hw", "st", "cc")}
        self.sems = {}
        self.maxwait = {}
        self.fenced = {}

    def raw(self, e, f):
        self.q[e].append(("plain", f))

    def wait(self, e, sem, val):
        if val is None:
            return
        if sem in self.DMA_SEMS:
            val = self.c[sem]
            if val <= 0:
                return
            self.maxwait[sem] = max(self.maxwait.get(sem, 0), val)
            self.raw(e, lambda eng, s=sem, v=val * 16: eng.wait_ge(self.sems[s], v))
        elif sem in self.CHAINED:
            assert isinstance(val, dict), (e, sem, val)
            self.raw(e, lambda eng, s=sem, m=val: eng.wait_ge(self.sems[s], m["v"]))
        else:
            if not isinstance(val, int) or val <= 0:
                return
            self.raw(e, lambda eng, s=sem, v=val: eng.wait_ge(self.sems[s], v))

    def step(self, e, f, inc=None, nw=True):
        if e in self.CHAINED and (inc is None or inc in self.CHAINED):
            import os as _os
            if _os.environ.get("K_NW_OFF", ""):
                off = _os.environ["K_NW_OFF"]
                if off == "all" or e in off:
                    nw = False
            m = {"v": None}
            self.q[e].append(("mark", f, m, nw))
            return m if inc is not None else None
        if inc is None:
            self.raw(e, f)
            return None
        if inc in self.DMA_SEMS:
            w = self.maxwait.get(inc, 0)
            if w > self.fenced.get(inc, 0):
                own = self.DMA_OWNER[inc]
                self.raw(own, lambda eng, s=inc, v=w * 16: eng.wait_ge(self.sems[s], v))
                self.fenced[inc] = w
        if inc in ("ld", "ld2"):
            for s2 in ("ld", "ld2"):
                if self.c[s2] > self.fenced.get("ring_" + s2, 0):
                    self.raw("G", lambda eng, s=s2, v=self.c[s2] * 16:
                             eng.wait_ge(self.sems[s], v))
                    self.fenced["ring_" + s2] = self.c[s2]
        if inc == "T" and self.c["T"] > 0:
            if getattr(f, "prefetch", False):
                prev = self.c["T"]
                inner = f

                def f(eng, inner=inner, prev=prev):
                    self._t_prev = prev
                    return inner(eng)
            else:
                self.raw(e, lambda eng, v=self.c["T"]: eng.wait_ge(self.sems["T"], v))
        self.c[inc] += 1
        mult = 16 if inc in self.DMA_SEMS else 1

        def g(eng, f=f, sem=inc, amt=mult):
            ins = f(eng)
            ins.then_inc(self.sems[sem], amt)
        self.raw(e, g)
        return self.c[inc]

    def tfence(self, eng):
        if getattr(self, "_t_prev", 0) > 0:
            eng.wait_ge(self.sems["T"], self._t_prev)
        self._t_prev = 0

    def mark_here(self, e):
        assert e in self.CHAINED
        m = {"v": None}
        self.q[e].append(("mark", None, m, False))
        return m

    def barrier(self, engines=("T", "V", "S", "G", "Y")):
        marks = {}
        for e in engines:
            if e == "Y":
                continue
            if e in self.CHAINED:
                marks[e] = ("chain", self.mark_here(e))
            else:
                self.c[e] += 1
                v = self.c[e]
                marks[e] = ("int", v)

                def f(eng, sem=e, v=v):
                    eng.nop().then_inc(self.sems[sem], 1)
                self.raw(e, f)
        for e in engines:
            for e2, (kind, v) in marks.items():
                if e2 == e:
                    continue
                self.wait(e, e2, v)

    def run_queue(self, e, eng):
        chained = isinstance(eng, (Chained, MockChained))
        for kind, *rest in self.q[e]:
            if kind == "plain":
                rest[0](eng)
            else:
                f, m, nw = rest
                if f is not None:
                    if chained:
                        eng._nw = nw
                    f(eng)
                    if chained:
                        eng._nw = False
                if chained:
                    m["v"] = eng._n

    def prepass(self):
        for e in self.CHAINED:
            self.run_queue(e, MockChained())


PHASE_ORDER = {"p0a": 0.1, "p0b": 0.2, "p0c": 0.3, "p0": 0.5, "a": 1, "exch": 2, "b": 3, "c": 4, "c2": 5, "d": 6}


def build_program(taps=(), stop_after="d", scratch=16384):
    PHASE_LVL = PHASE_ORDER[stop_after]
    nc = bass.Bass("TRN2", target_bir_lowering=False,
                   dynamic_dma_scratch_size=scratch)

    x_in = nc.dram_tensor("x", [TOK, D], F32, kind="ExternalInput").ap()
    gate_w = nc.dram_tensor("gate_w", [D, 2 * D], F32, kind="ExternalInput").ap()
    state_w = nc.dram_tensor("state_w", [D, D], F32, kind="ExternalInput").ap()
    out_w = nc.dram_tensor("out_w", [D, D], F32, kind="ExternalInput").ap()
    mlp_w1 = nc.dram_tensor("mlp_w1", [D, H], F32, kind="ExternalInput").ap()
    mlp_w2 = nc.dram_tensor("mlp_w2", [H, D], F32, kind="ExternalInput").ap()
    wmod_in = nc.dram_tensor("wmod", [D, 6 * D], F32, kind="ExternalInput").ap()
    cond_c = nc.dram_tensor("cond_c", [4, D], F32, kind="ExternalInput").ap()
    my_ffn_w = nc.dram_tensor("my_ffn_w", [D, D], F32, kind="ExternalInput").ap()
    vcols_in = nc.dram_tensor("vcols", [128, VC_NCOL], F32, kind="ExternalInput").ap()
    rows3 = nc.dram_tensor("rows3", [3, D], F32, kind="ExternalInput").ap()
    ident_in = nc.dram_tensor("ident", [128, 128], BF16, kind="ExternalInput").ap()
    mrow_in = nc.dram_tensor("mrow", [1, 8], F32, kind="ExternalInput").ap()
    invmrow_in = nc.dram_tensor("invmrow", [1, 8], F32, kind="ExternalInput").ap()
    mcols_in = nc.dram_tensor("mcols", [128, 8], F32, kind="ExternalInput").ap()

    out_hbm = nc.dram_tensor("out", [TOK, D], F32, kind="ExternalOutput").ap()
    tap_aps = {}
    for t in taps:
        shp, dt = {"xn": ([128, KC, SEG * NSEG], BF16),
                   "y": ([128, KC, TOK], BF16),
                   "xatt": ([128, 16, D], F32), "h2": ([128, KC, TOK], BF16),
                   "mod": ([4, 6 * D], BF16), "vin": ([1, NSEG, D], BF16),
                   "slin": ([128, NSEG, KC], F32), "xnw": ([128, 16, D], F32), "xn2": ([128, KC, SEG], BF16), "hid": ([128, 32, SEG], BF16), "w1q": ([128, KC, D], BF16)}[t]
        tap_aps[t] = nc.dram_tensor("tap_" + t, shp, dt, kind="ExternalOutput").ap()
    xn_tap_dram = (nc.dram_tensor("xn_tap_dram", [128, KC, TOK], BF16).ap()
                   if "xn" in taps else None)

    wT_bounce = nc.dram_tensor("wT_bounce", [128, KC * D], BF16)
    wT_all = nc.dram_tensor("wT_all", [NCORE * 128, KC * D], BF16)
    sl_bounce = nc.dram_tensor("sl_bounce", [128, 36], F32)
    sl_all = nc.dram_tensor("sl_all", [NCORE * 128, 36], F32)
    xnew_dram = nc.dram_tensor("xnew_dram", [TOK, D], F32).ap()

    E = Emitter()

    with ExitStack() as ctx:
        sb = lambda n, s, d: ctx.enter_context(nc.sbuf_tensor(n, s, d))
        ps = lambda n, s, d: ctx.enter_context(nc.psum_tensor(n, s, d))

        AW1 = sb("AW1", [128, KC, 2 * D], BF16)
        XNF = sb("XNF", [128, KC, 2 * D], BF16)
        YF = sb("YF", [128, KC, 2 * D], BF16)
        XATT = sb("XATT", [128, 16, D], F32)
        UF = sb("UF", [128, 3, SEG], BF16)
        FPA = sb("FPA", [128, 1, D], F32)
        FPB = sb("FPB", [128, 2, D], F32)
        ZTM = sb("ZTM", [128, D], BF16)
        BC = sb("BC", [128, 2, D], BF16)
        MOD = sb("MOD", [4, 6 * D], BF16)
        SCT = sb("SCT", [128, KC, 4], BF16)
        VC = sb("VC", [128, VC_NCOL], F32)
        MC = sb("MC", [128, 8], F32)
        BROW = sb("BROW", [1, D], BF16)
        IDT = sb("IDT", [128, 128], BF16)
        ONESB = sb("ONESB", [1, 128], BF16)
        ONESF = sb("ONESF", [1, 128], F32)
        ONESC = sb("ONESC", [128, 1], F32)
        MR = sb("MR", [1, 8], F32)
        IMR = sb("IMR", [1, 8], F32)
        CBSB = sb("CBSB", [128, 2, NSEG, KC], F32)
        SWS = sb("SWS", [128, KC], BF16)
        SWSF = sb("SWSF", [128, KC], F32)
        SBM = sb("SBM", [1, 1], F32)
        SUMA = sb("SUMA", [128, 16], F32)
        SQA = sb("SQA", [128, 16], F32)
        NMA = sb("NMA", [128, 16], F32)
        RSA = sb("RSA", [128, 16], F32)
        SUM2 = sb("SUM2", [128, 16], F32)
        SQ2 = sb("SQ2", [128, 16], F32)
        NM2 = sb("NM2", [128, 16], F32)
        RS2 = sb("RS2", [128, 16], F32)
        SUMB = sb("SUMB", [128, 16], F32)
        SQB = sb("SQB", [128, 16], F32)
        NMB = sb("NMB", [128, 16], F32)
        RSB = sb("RSB", [128, 16], F32)
        SUMH = sb("SUMH", [128, 2], F32)
        SQH = sb("SQH", [128, 2], F32)
        NM1 = sb("NM1", [128, 2], F32)
        RS1 = sb("RS1", [128, 2], F32)
        TMPC = sb("TMPC", [128, 16], F32)
        TMPD = sb("TMPD", [128, 16], F32)
        TMPE = sb("TMPE", [128, 16], F32)
        MMROWS = sb("MMROWS", [1, SEG], F32)
        DROW = sb("DROW", [1, SEG], F32)
        MODROW2 = MMROWS[0:1, :].bitcast(BF16)
        P_BF = sb("P_BF", [1, NSEG, SEG], BF16)
        SLST = sb("SLST", [128, NSEG, 9], F32)
        GATH = sb("GATH", [128, NCORE, 36], F32)
        SIN = sb("SIN", [128, NSEG, KC], F32)
        SINBF = sb("SINBF", [128, NSEG, KC], BF16)
        SLOCM = sb("SLOCM", [128, KC], F32)
        SINP = sb("SINP", [128, KC], F32)
        APR = sb("APR", [1, NSEG, 8], F32)
        VINR = XNF[0:1, 0:2, :].rearrange("p a (c d) -> p (a c) d", c=2)

        pA = [ps(f"pA{i}", [128, 512], F32) for i in range(4)]
        pB = [ps(f"pB{i}", [128, 512], F32) for i in range(2)]
        pC = ps("pC", [128, 512], F32)
        pT = ps("pT", [128, 1024], BF16)
        pT_F = pT[:, :].bitcast(F32)

        for name in ("T", "V", "S", "G", "ld", "ld2", "hw", "st", "cc"):
            E.sems[name] = ctx.enter_context(nc.semaphore("sem_" + name))

        XATT_BF_FLAT = XATT[:, :, :].bitcast(BF16).rearrange("p a b -> p (a b)")

        def Yap(s, k, j0, n):
            off = 8 * 2048 + s * 4096 + k * 512 + j0
            return XATT_BF_FLAT[:, off:off + n]

        SCBF = ZTM[0:4, :]
        ONESO = ONESF[:, :]
        state_wv = YF[:, :, 0:D]
        out_wv = YF[:, :, D:2 * D]
        HID = YF[:, :, :].rearrange("p a (b c) -> p (a b) c", b=4)
        wfT = AW1[:, :, 0:D]
        wiT = AW1[:, :, D:2 * D]
        wcT = XATT[:, 0:4, :].bitcast(BF16).rearrange("p a (b c) -> p (a b) c", b=2)
        ufT = XATT[:, 4:8, :].bitcast(BF16).rearrange("p a (b c) -> p (a b) c", b=2)
        uiT = YF[:, :, 0:D]
        ucT = YF[:, :, D:2 * D]
        woT = YF[:, :, 0:D]
        uoT = YF[:, :, D:2 * D]
        c_sb = XATT[:, 8:12, :].rearrange("p a (b c) -> p (a b) c", b=2)
        cbf = XATT[:, 12:14, :].bitcast(BF16).rearrange("p a (b c) -> p (a b) c", b=4)
        FIO = XATT[:, 14:16, :].rearrange("p a (b c) -> p (a b) c", b=2)
        zu_v = XATT[:, 8:12, :]
        XG = XATT[:, 0:2, :].bitcast(BF16).rearrange("p a (b c) -> p (a b) c", b=4)
        G2 = XATT[:, 2:4, :].bitcast(BF16).rearrange("p a (b c) -> p (a b) c", b=4)
        ZF = XATT[:, 4:6, :].bitcast(BF16).rearrange("p a (b c) -> p (a b) c", b=4)
        DA = XATT[:, 6, 0:512]
        DE = XATT[:, 6, 512:1024]
        TSTRIP = UF[:, 0:2, :].rearrange("p a b -> p (a b)")
        BMS = XATT[:, 7, 0:512]
        UST = XATT[:, 7, 512:1024].bitcast(BF16)[:, 0:512]
        CLA = FPB[0:4, 0, :]
        CLB = FPB[0:4, 1, :]
        CLC = FPA[0:4, 0, :]
        z2Fv = XNF[:, :, 0:512]
        xn2v = XNF[:, :, 512:1024]
        zbFv = AW1[:, :, 0:512]
        ffn_stage = AW1[:, :, 0:D]

        gd = lambda f: E.step("G", f, inc="ld")
        yd = lambda f: E.step("Y", f, inc="hw")

        ld_c = gd(lambda g: g.dma_start(out=SCBF[:, :], in_=cond_c[:, :]))
        ld_vc = yd(lambda y: y.dma_start(out=VC[:, :], in_=vcols_in[:, :]))
        ld_mc = yd(lambda y: y.dma_start(out=MC[:, :], in_=mcols_in[:, :]))
        ld_id = yd(lambda y: y.dma_start(out=IDT[:, :], in_=ident_in[:, :]))
        ld_mr = yd(lambda y: y.dma_start(out=MR[:, :], in_=mrow_in[:, :]))
        ld_imr = yd(lambda y: y.dma_start(out=IMR[:, :], in_=invmrow_in[:, :]))

        E.step("G", lambda g: g.memset(SLST[:, :, :], 0.0))
        E.step("G", lambda g: g.memset(ONESB[:, :], 1.0))
        E.step("G", lambda g: g.memset(ONESF[:, :], 1.0))
        E.step("G", lambda g: g.memset(ONESC[:, :], 1.0), inc="G")

        E.wait("S", "ld", ld_c)
        s_siluc = E.step("S", lambda s: s.activation(MOD[:, 0:D], SCBF[:, :], AF.Silu),
                         inc="S")
        E.wait("T", "S", s_siluc)
        E.wait("T", "hw", ld_id)

        def f_sct(t):
            ins = None
            for k in range(KC):
                ins = t.transpose(pT[:, k * 128:k * 128 + 4],
                                  MOD[0:4, k * 128:(k + 1) * 128], IDT[0:4, 0:4])
            return ins
        tv = E.step("T", f_sct, inc="T")
        E.wait("V", "T", tv)

        def f_sctev(v):
            ins = None
            for k in range(KC):
                ins = v.tensor_copy(SCT[:, k, :], pT[:, k * 128:k * 128 + 4])
            return ins
        v_sct = E.step("V", f_sctev, inc="V")
        E.wait("T", "V", v_sct)

        def colmath(SUM, SQ, NM, RS, c0, n, wv=()):
            for sem, val in wv:
                E.wait("V", sem, val)

            def f_v(v):
                v.tensor_scalar_mul(NM[:, c0:c0 + n], SUM[:, c0:c0 + n], -1.0 / D)
                v.tensor_mul(TMPC[:, c0:c0 + n], NM[:, c0:c0 + n], NM[:, c0:c0 + n])
                v.tensor_scalar(TMPD[:, c0:c0 + n], SQ[:, c0:c0 + n],
                                1.0 / D, float(EPS), ALU.mult, ALU.add)
                return v.tensor_sub(TMPE[:, c0:c0 + n], TMPD[:, c0:c0 + n],
                                    TMPC[:, c0:c0 + n])
            vm = E.step("V", f_v, inc="V", nw=False)
            E.wait("S", "V", vm)

            def f_s(s):
                s.activation(TMPC[:, c0:c0 + n], TMPE[:, c0:c0 + n], AF.Ln)
                return s.activation(RS[:, c0:c0 + n], TMPC[:, c0:c0 + n],
                                    AF.Exp, scale=-0.5)
            sm = E.step("S", f_s, inc="S", nw=False)
            return vm, sm

        def build_bcast(b, base, idx, plus1=False, wv=()):
            for e, sem, val in wv:
                E.wait(e, sem, val)

            def f_rex(t, b=b, base=base):
                t.matmul(pB[0][0:1, :], IDT[0:4, b:b + 1],
                         MOD[0:4, base: base + 512], start=True, stop=True)
                return t.matmul(pB[1][0:1, :], IDT[0:4, b:b + 1],
                                MOD[0:4, base + 512: base + 1024],
                                start=True, stop=True)
            tm = E.step("T", f_rex, inc="T")
            E.wait("S", "T", tm)
            sv = E.step("S", lambda s: (
                s.copy(MODROW2[0:1, 0:512], pB[0][0:1, :]),
                s.copy(MODROW2[0:1, 512:1024], pB[1][0:1, :]))[-1], inc="S")
            E.wait("T", "S", sv)
            vm = None
            for half in range(2):
                def f_o(t, half=half):
                    return t.matmul(pB[half][:, :], ONESB[:, :],
                                    MODROW2[0:1, half * 512:(half + 1) * 512],
                                    start=True, stop=True)
                tm = E.step("T", f_o, inc="T")
                E.wait("V", "T", tm)
                if plus1:
                    vm = E.step("V", lambda v, half=half, idx=idx: v.tensor_scalar_add(
                        BC[:, idx, half * 512:(half + 1) * 512], pB[half][:, :], 1.0),
                        inc="V")
                else:
                    vm = E.step("V", lambda v, half=half, idx=idx: v.tensor_copy(
                        BC[:, idx, half * 512:(half + 1) * 512], pB[half][:, :]),
                        inc="V")
                E.wait("T", "V", vm)
            return vm

        def build_CA_SA(b, wv=()):
            build_bcast(b, D, 0, plus1=True, wv=wv)
            return build_bcast(b, 0, 1)

        stat_sum = {}
        stat_sq = {}

        def emit_stats(seg):
            for t4 in range(4):
                t = seg * 4 + t4
                buf = XATT[:, t % 2, :]
                if t >= 2:
                    E.wait("Y", "V", stat_sum[t - 2])
                    E.wait("Y", "S", stat_sq[t - 2])
                hw_x = yd(lambda y, t=t, buf=buf: y.dma_start(
                    out=buf, in_=x_in[t * 128:(t + 1) * 128, :]))
                E.wait("V", "hw", hw_x)
                E.wait("S", "hw", hw_x)
                stat_sum[t] = E.step("V", lambda v, t=t, buf=buf: v.reduce_sum(
                    SUMA[:, t:t + 1], buf, axis=mybir.AxisListType.X), inc="V")
                stat_sq[t] = E.step("S", lambda sc, t=t, buf=buf: sc.activation(
                    ZTM[:, :], buf, AF.Square, accum_out=SQA[:, t:t + 1]), inc="S")
            colmath(SUMA, SQA, NMA, RSA, seg * 4, 4,
                    wv=[("S", stat_sq[seg * 4 + 3])])

        P2 = {"fh": None, "hwz": None}
        xn_mark = {}

        def pass2_gen(s, wv=(), zf_prev=None, xn_wait=None):
            build_CA_SA(s, wv=wv)
            if zf_prev is not None:
                E.wait("Y", "S", xn_mark[zf_prev])
            yield
            for tc in range(4):
                cidx = 4 * s + tc
                if P2["fh"] is not None:
                    E.wait("Y", "V", P2["fh"])
                r0 = s * SEG + tc * 128
                hw0 = yd(lambda y, r0=r0: y.dma_start(
                    out=FPA[:, 0, :], in_=x_in[r0:r0 + 128, :]))
                E.wait("V", "hw", hw0)

                def f_h(v, cidx=cidx):
                    v.tensor_scalar(FPB[:, 1, :], FPA[:, 0, :],
                                    NMA[:, cidx:cidx + 1], RSA[:, cidx:cidx + 1],
                                    ALU.add, ALU.mult)
                    v.tensor_mul(FPB[:, 0, :], FPB[:, 1, :], BC[:, 0, :])
                    return v.scalar_tensor_tensor(
                        FPB[:, 1, :], FPB[:, 0, :], 0.0, BC[:, 1, :],
                        ALU.add, ALU.add, accum_out=SUMH[:, 0:1])
                vmh = E.step("V", f_h, inc="V")
                P2["fh"] = vmh
                E.wait("S", "V", vmh)
                if P2["hwz"] is not None:
                    E.wait("S", "hw", P2["hwz"])
                smh = E.step("S", lambda sc: sc.activation(
                    ZTM[:, :], FPB[:, 1, :], AF.Square,
                    accum_out=SQH[:, 0:1]), inc="S")
                _, sm_cm1 = colmath(SUMH, SQH, NM1, RS1, 0, 1, wv=[("S", smh)])
                E.wait("V", "S", sm_cm1)
                vmz = E.step("V", lambda v: v.tensor_scalar(
                    ZTM[:, :], FPB[:, 1, :], NM1[:, 0:1], RS1[:, 0:1],
                    ALU.add, ALU.mult), inc="V")
                E.wait("Y", "V", vmz)
                hw_z = yd(lambda y, tc=tc: y.dma_start(
                    out=ZF[:, :, tc * 128:(tc + 1) * 128], in_=ZTM[:, :],
                    transpose=True))
                P2["hwz"] = hw_z
                E.wait("V", "hw", hw_z)
                yield
            E.wait("S", "hw", P2["hwz"])
            if xn_wait is not None:
                E.wait("S", "T", xn_wait)

            def f_xn(sc, s=s):
                ins = None
                for k in range(KC):
                    ins = sc.activation(XNF[:, k, s * SEG:(s + 1) * SEG], ZF[:, k, :],
                                        AF.Identity,
                                        bias=VC[:, VC_N1B + k:VC_N1B + k + 1],
                                        scale=VC[:, VC_N1W + k:VC_N1W + k + 1])
                return ins
            xn_mark[s] = E.step("S", f_xn, inc="S")

        def drain(g):
            if g is not None:
                for _ in g:
                    pass

        wmod_r = wmod_in.rearrange("(k p) o -> p k o", p=128)
        XNF01 = XNF[:, :, 0:D]
        mod_marks = []
        chunk_tmark = {}

        def emit_chunk(qi, wv=()):
            for e, sem, val in wv:
                E.wait(e, sem, val)
            buf = AW1[:, :, qi * D:(qi + 1) * D] if qi < 2 else XNF01
            if qi >= 3:
                E.wait("G", "T", chunk_tmark[qi - 1])
            ldq = gd(lambda g, qi=qi, buf=buf: g.dma_start(
                out=buf, in_=wmod_r[:, :, qi * D:(qi + 1) * D],
                max_dma_last_dim=8192))
            E.wait("T", "ld", ldq)
            for gi in range(2):
                def f_mod(t, gi=gi, buf=buf):
                    ins = None
                    for k in range(KC):
                        ins = t.matmul(pC[0:4, :], SCT[:, k, :],
                                       buf[:, k, gi * 512:(gi + 1) * 512],
                                       start=(k == 0), stop=(k == KC - 1))
                    return ins
                tvq = E.step("T", f_mod, inc="T")
                E.wait("S", "T", tvq)
                col0 = (qi * 2 + gi) * 512
                sv = E.step("S", lambda s, col0=col0: s.copy(
                    MOD[:, col0:col0 + 512], pC[0:4, :]), inc="S")
                mod_marks.append(sv)
                E.wait("T", "S", sv)
            chunk_tmark[qi] = E.step("T", lambda t: t.nop(), inc="T")

        emit_chunk(0)
        emit_stats(0)
        emit_stats(1)
        emit_chunk(1)
        emit_stats(2)
        emit_stats(3)
        E.wait("G", "T", chunk_tmark[1])
        ld_gate = gd(lambda g: g.dma_start(
            out=AW1[:, :, :], in_=gate_w.rearrange("(k p) o -> p k o", p=128),
            max_dma_last_dim=8192))
        ld_state = gd(lambda g: g.dma_start(
            out=state_wv, in_=state_w.rearrange("(k p) o -> p k o", p=128),
            max_dma_last_dim=8192))
        ld_outw = gd(lambda g: g.dma_start(
            out=out_wv, in_=out_w.rearrange("(k p) o -> p k o", p=128),
            max_dma_last_dim=8192))

        drain(pass2_gen(2, wv=[("T", "S", mod_marks[3])]))

        E.wait("T", "hw", ld_vc)
        t_sbm = E.step("T", lambda t: t.matmul(
            pB[0][0:1, 0:8], ONESC[:, :], VC[:, VC_STATEB:VC_STATEB + 8],
            start=True, stop=True), inc="T")
        E.wait("V", "T", t_sbm)
        E.step("V", lambda v: v.reduce_sum(SBM[:, :], pB[0][0:1, 0:8],
                                           axis=mybir.AxisListType.X), nw=False)
        v_sbm = E.step("V", lambda v: v.tensor_scalar_mul(SBM[:, :], SBM[:, :],
                                                          1.0 / D), inc="V", nw=False)
        E.wait("T", "V", v_sbm)

        E.wait("V", "ld", ld_state)

        def f_sws(v):
            for k in range(KC):
                v.reduce_sum(SWSF[:, k:k + 1], state_wv[:, k, :],
                             axis=mybir.AxisListType.X)
            return v.tensor_copy(SWS[:, :], SWSF[:, :])
        v_sws = E.step("V", f_sws, inc="V", nw=False)

        evacG = {}
        xg_last = {}
        u_last = {}
        uS_last = {}
        uV_last = {}
        scan_last = {}
        de_mark = {}
        da_mark = {}
        fsl = {}

        def emit_amm(s, prev=None, pc_wait=None, filler=None):
            if prev is None:
                E.wait("T", "ld", ld_gate)
            E.wait("T", "S", xn_mark[s])
            xg_prev = None
            for o in range(16):
                if filler is not None and o in (1, 3, 5, 7, 9):
                    next(filler, None)
                bank = pA[o % 4]
                pv = evacG.get((s, o - 4), evacG.get((prev, o + 12)))
                if pv is not None:
                    E.wait("T", "S", pv)

                def f_g(t, o=o, bank=bank, s=s):
                    ins = None
                    for k in range(KC):
                        if k == KC - 1:
                            E.tfence(t)
                        ins = t.matmul(bank[:, :], AW1[:, k, o * 128:(o + 1) * 128],
                                       XNF[:, k, s * SEG:(s + 1) * SEG],
                                       start=(k == 0), stop=(k == KC - 1))
                    return ins
                f_g.prefetch = True
                tm = E.step("T", f_g, inc="T")
                E.wait("S", "T", tm)
                if o < 8:
                    if xg_prev is not None:
                        E.wait("S", "V", xg_prev)
                    elif prev is not None:
                        E.wait("S", "V", xg_last[prev])
                    dst = UF[:, 2, :]
                else:
                    if o == 8 and prev is not None:
                        E.wait("S", "T", u_last[prev])
                    dst = G2[:, o - 8, :]
                sm = E.step("S", lambda sc, bank=bank, dst=dst, o=o: sc.activation(
                    dst, bank[:, :], AF.Silu,
                    bias=VC[:, VC_GATEB + o:VC_GATEB + o + 1]), inc="S")
                evacG[(s, o)] = sm
                if o < 8:
                    E.wait("V", "S", sm)
                    if o == 0 and prev is not None:
                        E.wait("V", "T", u_last[prev])
                    xg_prev = E.step("V", lambda v, o=o, s=s: v.tensor_mul(
                        XG[:, o, :], XNF[:, o, s * SEG:(s + 1) * SEG], UF[:, 2, :]),
                        inc="V")
            xg_last[s] = xg_prev

            if prev is None:
                E.wait("T", "V", v_sws)
            else:
                E.wait("T", "V", de_mark[prev])
                E.wait("T", "S", da_mark[prev])
                E.wait("T", "S", uS_last[prev])
            if pc_wait is not None:
                E.wait("T", "S", pc_wait)

            def f_mv(t, s=s):
                ins = None
                for k in range(KC):
                    ins = t.matmul(pC[0:1, :], SWS[:, k:k + 1],
                                   XNF[:, k, s * SEG:(s + 1) * SEG],
                                   start=(k == 0), stop=(k == KC - 1))
                return ins
            tm_mv = E.step("T", f_mv, inc="T")
            E.wait("S", "T", tm_mv)
            if prev is None:
                E.wait("S", "V", v_sbm)
            else:
                E.wait("S", "V", fsl[prev])

            def f_sp(sc):
                sc.activation(DROW[:, :], pC[0:1, :], AF.Exp, scale=1.0 / D,
                              bias=SBM[0:1, 0:1])
                return sc.activation(MMROWS[:, :], DROW[:, :], AF.Ln, bias=1.0)
            sm_sp = E.step("S", f_sp, inc="S", nw=False)
            E.wait("T", "S", sm_sp)
            E.wait("V", "S", sm_sp)
            vm_cs = E.step("V", lambda v: v.tensor_tensor_scan(
                DROW[:, :], MMROWS[:, :], MMROWS[:, :], 0.0,
                ALU.add, ALU.bypass), inc="V")
            E.wait("S", "V", vm_cs)
            sm_p = E.step("S", lambda sc, s=s: sc.activation(
                P_BF[:, s, :], DROW[:, :], AF.Exp, scale=-1.0), inc="S")

            def f_dade(t):
                return t.matmul(pC[:, :], ONESO, MMROWS[:, :], start=True,
                                stop=True)
            tm_dd = E.step("T", f_dade, inc="T")
            E.wait("V", "T", tm_dd)
            vm_de = E.step("V", lambda v: v.tensor_copy(DE, pC[:, :]), inc="V")
            E.wait("S", "V", vm_de)
            if prev is not None:
                E.wait("S", "V", scan_last[prev])
            sm_da = E.step("S", lambda sc: sc.activation(
                DA, pC[:, :], AF.Exp, scale=-1.0), inc="S")
            E.wait("T", "V", vm_de)
            E.wait("T", "S", sm_da)
            E.wait("G", "S", sm_da)
            de_mark[s] = vm_de
            da_mark[s] = sm_da

            E.wait("T", "V", xg_last[s])
            E.wait("T", "S", evacG[(s, 15)])
            if prev is None:
                E.wait("T", "ld", ld_state)
            gm = None
            vm_sc = None
            for o in range(KC):
                ub = o % 2

                def f_u(t, o=o):
                    ins = None
                    for k in range(KC):
                        t.matmul(pB[0][:, :], state_wv[:, k, o * 128:(o + 1) * 128],
                                 XG[:, k, :], start=(k == 0), stop=(k == KC - 1))
                        if k == KC - 1:
                            E.tfence(t)
                        ins = t.matmul(pB[1][:, :],
                                       state_wv[:, k, o * 128:(o + 1) * 128],
                                       G2[:, k, :], start=(k == 0),
                                       stop=(k == KC - 1))
                    return ins
                f_u.prefetch = True
                tm = E.step("T", f_u, inc="T")
                E.wait("S", "T", tm)
                sm_u = E.step("S", lambda sc, o=o: sc.activation(
                    UST[:, :], pB[0][:, :], AF.Identity,
                    bias=VC[:, VC_STATEB + o:VC_STATEB + o + 1]), inc="S")
                E.wait("V", "T", tm)
                vm_b = E.step("V", lambda v, o=o: v.scalar_tensor_tensor(
                    BMS[:, :], pB[1][:, :],
                    VC[:, VC_STATEB + o:VC_STATEB + o + 1],
                    DE, ALU.add, ALU.mult), inc="V")
                E.wait("G", "S", sm_u)
                E.wait("G", "V", vm_b)
                if o == 0 and prev is not None:
                    E.wait("G", "V", scan_last[prev])
                gm = E.step("G", lambda g, ub=ub: g.tensor_mul(
                    UF[:, ub, :], BMS[:, :], UST[:, :]), inc="G")
                E.wait("V", "G", gm)
                E.wait("T", "G", gm)
                vm_sc = E.step("V", lambda v, o=o, s=s, ub=ub: v.tensor_tensor_scan(
                    Yap(s, o, 0, SEG), DA, UF[:, ub, :], 0.0,
                    ALU.mult, ALU.add), inc="V")
                if o >= 1:
                    E.wait("G", "V", vm_sc)
                uS_last[s] = sm_u
                uV_last[s] = vm_b
                u_last[s] = tm
            scan_last[s] = vm_sc

            def f_sl(v, s=s):
                for k in range(KC):
                    v.tensor_copy(SLST[:, s, k:k + 1], Yap(s, k, 511, 1))
                return v.tensor_copy(SLST[0:1, s, 8:9], DROW[0:1, 511:512])
            fsl[s] = E.step("V", f_sl, inc="V")
            drain(filler)

        import os as _os
        K_NOFILL = _os.environ.get("K_NOFILL", "") == "1"

        def mkfill(g):
            if K_NOFILL:
                drain(g)
                return None
            return g

        if PHASE_LVL >= 1:
            emit_amm(2, filler=mkfill(pass2_gen(3, zf_prev=2)))
            emit_chunk(2, wv=[("T", "V", de_mark[2]), ("T", "S", da_mark[2]),
                              ("T", "S", uS_last[2])])
            emit_chunk(3)
            emit_chunk(4)
            emit_chunk(5)
            emit_amm(3, prev=2, pc_wait=mod_marks[-1],
                     filler=mkfill(pass2_gen(0, wv=[("T", "S", uS_last[2]),
                                             ("T", "V", uV_last[2])],
                                      zf_prev=3, xn_wait=chunk_tmark[5])))
            emit_amm(0, prev=3,
                     filler=mkfill(pass2_gen(1, wv=[("T", "S", uS_last[3]),
                                             ("T", "V", uV_last[3])],
                                      zf_prev=0)))
            emit_amm(1, prev=0)

        if PHASE_LVL >= 2:
            E.barrier()
            if "y" in tap_aps:
                for s in range(NSEG):
                    for k in range(KC):
                        yd(lambda y, s=s, k=k: y.dma_start(
                            out=tap_aps["y"][:, k, s * SEG:(s + 1) * SEG],
                            in_=Yap(s, k, 0, SEG)))
            hw_sl = yd(lambda y: y.dma_start(
                out=sl_bounce[:, :], in_=SLST.rearrange("p a b -> p (a b)")))
            E.wait("G", "hw", hw_sl)
            cc_sl = E.step("G", lambda g: g.collective_compute(
                "AllGather", ALU.bypass, replica_groups=[list(range(NCORE))],
                ins=[sl_bounce.ap().opt()], outs=[sl_all.ap().opt()]), inc="cc")
            E.wait("G", "cc", cc_sl)
            ld_ga = gd(lambda g: g.dma_start(
                out=GATH[:, :, :], in_=sl_all.ap().rearrange("(j p) v -> p j v", p=128)))
            E.wait("S", "ld", ld_ga)
            E.wait("V", "ld", ld_ga)

            def f_apr(sc):
                ins = None
                for b in range(NSEG):
                    src = GATH[0:1, :, b * 9 + 8:b * 9 + 9].rearrange("p j v -> p (j v)")
                    ins = sc.activation(APR[:, b, :], src, AF.Exp, scale=-1.0)
                return ins
            sm_apr = E.step("S", f_apr, inc="S")
            E.wait("V", "S", sm_apr)
            E.wait("V", "hw", ld_imr)

            def f_aprm(v):
                ins = None
                for b in range(NSEG):
                    v.tensor_mul(SLOCM[0:1, :], APR[:, b, :], MR[:, :])
                    ins = v.tensor_add(APR[:, b, :], SLOCM[0:1, :], IMR[:, :])
                return ins
            vm_aprm = E.step("V", f_aprm, inc="V", nw=False)
            E.wait("T", "V", vm_aprm)

            def f_apc(t):
                ins = None
                for b in range(NSEG):
                    for j in range(NCORE):
                        ins = t.matmul(pC[:, b * 8 + j:b * 8 + j + 1], ONESO,
                                       APR[:, b, j:j + 1], start=True, stop=True)
                return ins
            tm_apc = E.step("T", f_apc, inc="T")
            E.wait("V", "T", tm_apc)

            def f_fold(v):
                ins = None
                for b in range(NSEG):
                    v.memset(SINP[:, :], 0.0)
                    cur, nxt = SINP[:, :], SIN[:, b, :]
                    for j in range(NCORE):
                        v.tensor_scalar_mul(SLOCM[:, :], GATH[:, j, b * 9:b * 9 + 8],
                                            MC[:, j:j + 1])
                        v.scalar_tensor_tensor(nxt, cur,
                                               pC[:, b * 8 + j:b * 8 + j + 1],
                                               SLOCM[:, :], ALU.mult, ALU.add)
                        cur, nxt = nxt, cur
                    v.tensor_copy(SIN[:, b, :], SINP[:, :])
                    ins = v.tensor_copy(SINBF[:, b, :], SIN[:, b, :])
                return ins
            vm_fold = E.step("V", f_fold, inc="V", nw=False)

            E.wait("T", "V", vm_fold)
            E.wait("T", "ld", ld_outw)
            for b in range(NSEG):
                for jg in range(2):
                    def f_vi(t, b=b, jg=jg):
                        ins = None
                        for k in range(KC):
                            ins = t.matmul(pB[0][0:1, :], SINBF[:, b, k:k + 1],
                                           out_wv[:, k, jg * 512:(jg + 1) * 512],
                                           start=(k == 0), stop=(k == KC - 1))
                        return ins
                    tm = E.step("T", f_vi, inc="T")
                    E.wait("S", "T", tm)
                    sm = E.step("S", lambda sc, b=b, jg=jg: sc.copy(
                        VINR[:, b, jg * 512:(jg + 1) * 512], pB[0][0:1, :]), inc="S")
                    E.wait("T", "S", sm)

            ld_brow = gd(lambda g: g.dma_start(out=BROW[:, :], in_=rows3[0:1, :]))
            E.barrier()

        if PHASE_LVL >= 3:
            import os as _os
            B_SEGS = int(_os.environ.get("K_B_SEGS", "4"))
            B_PARTS = int(_os.environ.get("K_B_PARTS", "7"))
            banks8 = pA + pB + [pC, pT_F]
            E.wait("T", "ld", ld_brow)
            for s in range(B_SEGS):
                if B_PARTS & 1:
                    vm_bc = build_CA_SA(s)
                for tc in range(4 if (B_PARTS & 2) else 0):
                    for jg in range(2):
                        bank = banks8[tc * 2 + jg]

                        def f_op(t, s=s, tc=tc, jg=jg, bank=bank):
                            ins = None
                            for k in range(KC):
                                ins = t.matmul(bank[:, :], Yap(s, k, tc * 128, 128),
                                               out_wv[:, k, jg * 512:(jg + 1) * 512],
                                               start=(k == 0), stop=False)
                            t.matmul(bank[:, :], ONESB[:, :],
                                     BROW[:, jg * 512:(jg + 1) * 512],
                                     start=False, stop=False)
                            E.tfence(t)
                            return t.matmul(bank[:, :],
                                            P_BF[:, s, tc * 128:(tc + 1) * 128],
                                            VINR[:, s, jg * 512:(jg + 1) * 512],
                                            start=False, stop=True)
                        f_op.prefetch = True
                        tm_op = E.step("T", f_op, inc="T")
                if B_PARTS & 2:
                    E.wait("V", "T", tm_op)

                for tc in range(4 if (B_PARTS & 4) else 0):
                    cidx = 4 * s + tc
                    r0 = s * SEG + tc * 128
                    hw0 = yd(lambda y, r0=r0: y.dma_start(
                        out=FPA[:, 0, :], in_=x_in[r0:r0 + 128, :]))
                    E.wait("V", "hw", hw0)

                    def f_hp(v, cidx=cidx):
                        v.tensor_scalar(FPB[:, 0, :], FPA[:, 0, :],
                                        NMA[:, cidx:cidx + 1], RSA[:, cidx:cidx + 1],
                                        ALU.add, ALU.mult)
                        v.tensor_mul(FPB[:, 1, :], FPB[:, 0, :], BC[:, 0, :])
                        return v.tensor_add(FPB[:, 0, :], FPB[:, 1, :], BC[:, 1, :])
                    vm_hp = E.step("V", f_hp, inc="V")
                    E.wait("Y", "V", vm_hp)

                    def f_xa(v, s=s, tc=tc):
                        for jg in range(2):
                            v.scalar_tensor_tensor(
                                XATT[:, 4 * s + tc, jg * 512:(jg + 1) * 512],
                                banks8[tc * 2 + jg][:, :], 0.0,
                                FPB[:, 0, jg * 512:(jg + 1) * 512],
                                ALU.add, ALU.add, accum_out=SUMH[:, jg:jg + 1])
                        return v.tensor_add(SUM2[:, 4 * s + tc:4 * s + tc + 1],
                                            SUMH[:, 0:1], SUMH[:, 1:2])
                    vm_xa = E.step("V", f_xa, inc="V", nw=False)
                    E.wait("S", "V", vm_xa)
                    sm = E.step("S", lambda sc, s=s, tc=tc: sc.activation(
                        ZTM[:, :], XATT[:, 4 * s + tc, :], AF.Square,
                        accum_out=SQ2[:, 4 * s + tc:4 * s + tc + 1]), inc="S")
                E.wait("T", "V", vm_xa)

            E.barrier()
            ld_brow2 = gd(lambda g: g.dma_start(out=BROW[:, :], in_=rows3[1:2, :]))
            if "xatt" in tap_aps:
                yd(lambda y: y.dma_start(out=tap_aps["xatt"], in_=XATT[:, :, :]))
            if B_PARTS & 4:
                _, sm_cm2 = colmath(SUM2, SQ2, NM2, RS2, 0, 16)
                E.wait("V", "S", sm_cm2)

            E.wait("G", "T", E.c["T"])
            ld_ffnw = gd(lambda g: g.dma_start(
                out=ffn_stage, in_=my_ffn_w.rearrange("(k p) o -> p k o", p=128),
                max_dma_last_dim=8192))
            E.wait("T", "ld", ld_ffnw)
            hw_wtb = None
            tw = None
            vcp = None
            for ki in range(KC):
                def f_wt(t, ki=ki):
                    ins = None
                    for km in range(KC):
                        ins = t.transpose(pT[:, km * 128:(km + 1) * 128],
                                          AW1[:, km, ki * 128:(ki + 1) * 128],
                                          IDT[:, :])
                    return ins
                if vcp is not None:
                    E.wait("T", "V", vcp)
                tw = E.step("T", f_wt, inc="T")
                E.wait("V", "T", tw)
                if hw_wtb is not None:
                    E.wait("V", "hw", hw_wtb)
                vcp = E.step("V", lambda v: v.tensor_copy(TSTRIP[:, :], pT[:, :]),
                             inc="V")
                E.wait("Y", "V", vcp)
                hw_wtb = yd(lambda y, ki=ki: y.dma_start(
                    out=wT_bounce.ap()[:, ki * D:(ki + 1) * D], in_=TSTRIP[:, :]))
            E.wait("T", "V", vcp)
            E.wait("G", "hw", hw_wtb)
            cc_wt = E.step("G", lambda g: g.collective_compute(
                "AllGather", ALU.bypass, replica_groups=[list(range(NCORE))],
                ins=[wT_bounce.ap().opt()], outs=[wT_all.ap().opt()]), inc="cc")

            clamp_vv = None
            for r0 in (2 * D, 5 * D):
                sva = E.step("S", lambda s, r0=r0: s.activation(
                    CLA, MOD[:, r0:r0 + D], AF.Abs), inc="S")
                svb = E.step("S", lambda s, r0=r0: s.activation(
                    CLB, MOD[:, r0:r0 + D], AF.Sign), inc="S")
                E.wait("V", "S", svb)
                E.step("V", lambda v: v.tensor_scalar_max(CLC, CLA, 0.25))
                vv = E.step("V", lambda v, r0=r0: v.tensor_mul(
                    MOD[:, r0:r0 + D], CLC, CLB), inc="V")
                E.wait("S", "V", vv)
                clamp_vv = vv
            E.wait("T", "V", clamp_vv)
            E.wait("Y", "V", clamp_vv)

            for vi, base in enumerate((4 * D, 3 * D)):
                for b in range(NSEG):
                    def f_rex2(t, base=base, b=b):
                        t.matmul(pB[0][0:1, :], IDT[0:4, b:b + 1],
                                 MOD[0:4, base: base + 512], start=True, stop=True)
                        return t.matmul(pB[1][0:1, :], IDT[0:4, b:b + 1],
                                        MOD[0:4, base + 512: base + 1024],
                                        start=True, stop=True)
                    tvq = E.step("T", f_rex2, inc="T")
                    E.wait("S", "T", tvq)
                    sv = E.step("S", lambda s: (
                        s.copy(MODROW2[0:1, 0:512], pB[0][0:1, :]),
                        s.copy(MODROW2[0:1, 512:1024], pB[1][0:1, :]))[-1], inc="S")
                    E.wait("T", "S", sv)

                    def f_ext(t):
                        ins = None
                        for k in range(KC):
                            ins = t.matmul(pC[:, k:k + 1],
                                           MODROW2[0:1, k * 128:(k + 1) * 128],
                                           ONESB[:, 0:1], start=True, stop=True)
                        return ins
                    tvq = E.step("T", f_ext, inc="T")
                    E.wait("V", "T", tvq)
                    if vi == 0:
                        vv = E.step("V", lambda v, b=b: v.tensor_scalar_add(
                            CBSB[:, 0, b, :], pC[:, 0:KC], 1.0), inc="V")
                    else:
                        vv = E.step("V", lambda v, b=b: v.tensor_copy(
                            CBSB[:, 1, b, :], pC[:, 0:KC]), inc="V")
                    E.wait("T", "V", vv)

        if PHASE_LVL >= 4:
            buf_lastT = {0: tw, 1: tw}
            last_xnew_v = {}
            q_bufs = (AW1[:, 0:4, :].rearrange("p a (b c) -> p (a b) c", b=2),
                      AW1[:, 4:8, :].rearrange("p a (b c) -> p (a b) c", b=2))
            for s in range(NSEG):
                vm_ga = build_bcast(s, 2 * D, 0)

                sq_last = None
                for tc in range(4):
                    vm_z2 = E.step("V", lambda v, s=s, tc=tc: v.tensor_scalar(
                        ZTM[:, :], XATT[:, 4 * s + tc, :],
                        NM2[:, 4 * s + tc:4 * s + tc + 1],
                        RS2[:, 4 * s + tc:4 * s + tc + 1], ALU.add, ALU.mult), inc="V")
                    E.wait("Y", "V", vm_z2)
                    hw_z = yd(lambda y, tc=tc: y.dma_start(
                        out=z2Fv[:, :, tc * 128:(tc + 1) * 128], in_=ZTM[:, :],
                        transpose=True))
                    E.wait("V", "hw", hw_z)
                E.wait("S", "hw", E.c["hw"])

                def f_xn2(sc):
                    ins = None
                    for k in range(KC):
                        ins = sc.activation(xn2v[:, k, :], z2Fv[:, k, :], AF.Identity,
                                            bias=VC[:, VC_N2B + k:VC_N2B + k + 1],
                                            scale=VC[:, VC_N2W + k:VC_N2W + k + 1])
                    return ins
                sm_xn2 = E.step("S", f_xn2, inc="S")
                if s == 0 and "xn2" in tap_aps:
                    E.wait("Y", "S", sm_xn2)
                    yd(lambda y: y.dma_start(out=tap_aps["xn2"], in_=xn2v[:, :, :]))
                E.wait("T", "S", sm_xn2)

                evm = {}
                for q in range(4):
                    buf = q_bufs[q % 2]
                    if buf_lastT.get(q % 2):
                        E.wait("G", "T", buf_lastT[q % 2])
                    ldq = gd(lambda g, q=q, buf=buf: g.dma_start(
                        out=buf,
                        in_=mlp_w1.rearrange("(k p) o -> p k o", p=128)[:, :, q * 1024:(q + 1) * 1024],
                        max_dma_last_dim=8192))
                    E.wait("T", "ld", ldq)
                    if s == 0 and q == 0 and "w1q" in tap_aps:
                        E.wait("Y", "ld", ldq)
                        yd(lambda y, buf=buf: y.dma_start(out=tap_aps["w1q"], in_=buf))
                    for hc in range(8):
                        hidx = q * 8 + hc
                        bank = pA[hidx % 4]
                        if hidx >= 4:
                            E.wait("T", "S", evm[hidx - 4])

                        def f_m1(t, hc=hc, bank=bank, buf=buf):
                            ins = None
                            for k in range(KC):
                                if k == KC - 1:
                                    E.tfence(t)
                                ins = t.matmul(bank[:, :], buf[:, k, hc * 128:(hc + 1) * 128],
                                               xn2v[:, k, :], start=(k == 0),
                                               stop=(k == KC - 1))
                            return ins
                        f_m1.prefetch = True
                        tm = E.step("T", f_m1, inc="T")
                        buf_lastT[q % 2] = tm
                        E.wait("S", "T", tm)
                        sm = E.step("S", lambda sc, hidx=hidx, bank=bank: sc.activation(
                            HID[:, hidx, :], bank[:, :], AF.Gelu,
                            bias=VC[:, VC_MLPB1 + hidx:VC_MLPB1 + hidx + 1]), inc="S")
                        evm[hidx] = sm
                E.wait("T", "S", evm[31])
                if s == 0 and "hid" in tap_aps:
                    E.wait("Y", "S", evm[31])
                    yd(lambda y: y.dma_start(out=tap_aps["hid"], in_=HID[:, :, :]))

                for q in range(4):
                    buf = q_bufs[q % 2]
                    if buf_lastT.get(q % 2):
                        E.wait("G", "T", buf_lastT[q % 2])
                    ldq = gd(lambda g, q=q, buf=buf: g.dma_start(
                        out=buf,
                        in_=mlp_w2.rearrange("(k p) o -> p k o", p=128)[:, q * 8:(q + 1) * 8, :],
                        max_dma_last_dim=8192))
                    E.wait("T", "ld", ldq)

                    def f_m2(t, q=q, buf=buf):
                        ins = None
                        for tc in range(4):
                            for jg in range(2):
                                bank = banks8[tc * 2 + jg]
                                for k8 in range(8):
                                    if tc == 3 and jg == 1 and k8 == 7:
                                        E.tfence(t)
                                    ins = t.matmul(bank[:, :],
                                                   HID[:, q * 8 + k8, tc * 128:(tc + 1) * 128],
                                                   buf[:, k8, jg * 512:(jg + 1) * 512],
                                                   start=(q == 0 and k8 == 0), stop=False)
                        return ins
                    f_m2.prefetch = True
                    tmq = E.step("T", f_m2, inc="T")
                    buf_lastT[q % 2] = tmq

                E.wait("T", "ld", ld_brow2)

                def f_m2b(t):
                    ins = None
                    for tc in range(4):
                        for jg in range(2):
                            if tc == 3 and jg == 1:
                                E.tfence(t)
                            ins = t.matmul(banks8[tc * 2 + jg][:, :], ONESB[:, :],
                                           BROW[:, jg * 512:(jg + 1) * 512],
                                           start=False, stop=True)
                    return ins
                f_m2b.prefetch = True
                tm_m2 = E.step("T", f_m2b, inc="T")
                E.wait("V", "T", tm_m2)

                for tc in range(4):
                    r0 = s * SEG + tc * 128
                    if last_xnew_v.get(0):
                        E.wait("Y", "V", last_xnew_v[0])
                    hw_x = yd(lambda y, r0=r0: y.dma_start(
                        out=FPA[:, 0, :], in_=x_in[r0:r0 + 128, :]))

                    def f_a2(v, tc=tc, s=s):
                        for jg in range(2):
                            v.tensor_add(FPB[:, 0, jg * 512:(jg + 1) * 512],
                                         banks8[tc * 2 + jg][:, :],
                                         XATT[:, 4 * s + tc, jg * 512:(jg + 1) * 512])
                        return v.engine_nop()
                    vm_a2 = E.step("V", f_a2, inc="V")
                    gm = E.step("V", lambda v: v.tensor_mul(
                        FPB[:, 1, :], FPB[:, 0, :], BC[:, 0, :]), inc="V")
                    E.wait("V", "hw", hw_x)
                    vm_xn = E.step("V", lambda v, s=s, tc=tc: v.scalar_tensor_tensor(
                        XATT[:, 4 * s + tc, :], FPB[:, 1, :], 0.0, FPA[:, 0, :],
                        ALU.add, ALU.add,
                        accum_out=SUMB[:, 4 * s + tc:4 * s + tc + 1]), inc="V")
                    last_xnew_v[0] = vm_xn
                    E.wait("S", "V", vm_xn)
                    sm = E.step("S", lambda sc, s=s, tc=tc: sc.activation(
                        ZTM[:, :], XATT[:, 4 * s + tc, :], AF.Square,
                        accum_out=SQB[:, 4 * s + tc:4 * s + tc + 1]), inc="S")
                    E.wait("T", "V", vm_xn)

            E.barrier()
            _, sm_cmb = colmath(SUMB, SQB, NMB, RSB, 0, 16)
            E.wait("V", "S", sm_cmb)

        if PHASE_LVL >= 5:
            hw_bounce = None
            sm_h2 = None
            for s in range(NSEG):
                if sm_h2 is not None:
                    E.wait("Y", "S", sm_h2)
                for tc in range(4):
                    vm_zb = E.step("V", lambda v, s=s, tc=tc: v.tensor_scalar(
                        ZTM[:, :], XATT[:, 4 * s + tc, :],
                        NMB[:, 4 * s + tc:4 * s + tc + 1],
                        RSB[:, 4 * s + tc:4 * s + tc + 1], ALU.add, ALU.mult), inc="V")
                    E.wait("Y", "V", vm_zb)
                    hw_z = yd(lambda y, tc=tc: y.dma_start(
                        out=zbFv[:, :, tc * 128:(tc + 1) * 128], in_=ZTM[:, :],
                        transpose=True))
                    E.wait("V", "hw", hw_z)
                E.wait("S", "hw", E.c["hw"])

                def f_h2(sc, s=s):
                    ins = None
                    for k in range(KC):
                        ins = sc.activation(XNF[:, k, s * SEG:(s + 1) * SEG],
                                            zbFv[:, k, :], AF.Identity,
                                            bias=CBSB[:, 1, s, k:k + 1],
                                            scale=CBSB[:, 0, s, k:k + 1])
                    return ins
                sm_h2 = E.step("S", f_h2, inc="S")
                for tc in range(4):
                    r0 = s * SEG + tc * 128
                    hw_bounce = yd(lambda y, r0=r0, s=s, tc=tc: y.dma_start(
                        out=xnew_dram[r0:r0 + 128, :], in_=XATT[:, 4 * s + tc, :]))

            E.barrier()
            E.wait("Y", "hw", hw_bounce)
            E.wait("G", "hw", hw_bounce)

        if "h2" in tap_aps:
            def f_taph(y):
                return y.dma_start(out=tap_aps["h2"], in_=XNF[:, :, :])
            E.step("Y", f_taph, inc="hw")
        if "mod" in tap_aps:
            def f_tapm(y):
                return y.dma_start(out=tap_aps["mod"], in_=MOD[:, :])
            E.step("Y", f_tapm, inc="hw")
        if "vin" in tap_aps:
            yd(lambda y: y.dma_start(out=tap_aps["vin"], in_=VINR[0:1, :, :]))
        if "slin" in tap_aps:
            yd(lambda y: y.dma_start(out=tap_aps["slin"], in_=SIN[:, :, :]))
        if "xn" in tap_aps:
            yd(lambda y: y.dma_start(out=tap_aps["xn"], in_=xn_tap_dram))
        if taps:
            E.barrier()

        if PHASE_LVL < 6:
            E.barrier()
            st = E.step("Y", lambda y: y.dma_start(
                out=out_hbm[0:128, :], in_=FPA[:, 0, :]), inc="st")
        if PHASE_LVL >= 6:
            ld_brow3 = gd(lambda g: g.dma_start(out=BROW[:, :], in_=rows3[2:3, :]))
            E.wait("Y", "cc", cc_wt)
            d_slots = [("wf", wfT), ("uf", ufT), ("wi", wiT), ("ui", uiT),
                       ("wc", wcT), ("uc", ucT)]
            slot_idx = {"wf": 0, "uf": 1, "wi": 2, "ui": 3, "wc": 4, "uc": 5,
                        "wo": 6, "uo": 7}
            hw_gate_w = None
            for name, dst in d_slots:
                j = slot_idx[name]
                hw_gate_w = yd(lambda y, j=j, dst=dst: y.dma_start(
                    out=dst,
                    in_=wT_all.ap()[j * 128:(j + 1) * 128, :].rearrange(
                        "p (a b) -> p a b", a=KC)))

            def f_cinit(v):
                ins = None
                for k in range(KC):
                    ins = v.tensor_scalar(c_sb[:, k, :], XNF[:, k, 0:512],
                                          0.0, VC[:, VC_INC + k:VC_INC + k + 1],
                                          ALU.mult, ALU.add)
                return ins
            E.wait("V", "hw", 1)
            vm_c0 = E.step("V", f_cinit, inc="V")
            E.wait("G", "V", vm_c0)

            E.wait("T", "hw", hw_gate_w)
            gate_defs = [(wfT, ufT, VC_BF, AF.Sigmoid, 0),
                         (wiT, uiT, VC_BI, AF.Sigmoid, 1),
                         (wcT, ucT, VC_BC, AF.Tanh, 2)]
            gm = None
            for t_step in range(NSEG):
                def f_cc(sc):
                    ins = None
                    for k in range(KC):
                        ins = sc.copy(cbf[:, k, :], c_sb[:, k, :])
                    return ins
                if t_step > 0:
                    E.wait("S", "G", gm)
                else:
                    E.wait("S", "V", vm_c0)
                gm_cc = E.step("S", f_cc, inc="S")
                E.wait("T", "S", gm_cc)

                prev_evac = {}
                for o in range(KC):
                    for wT_, uT_, vb, af, slot in gate_defs:
                        bank = pA[slot]

                        def f_gate(t, wT_=wT_, uT_=uT_, o=o, bank=bank, t_step=t_step):
                            ins = None
                            for k in range(KC):
                                ins = t.matmul(bank[:, :], wT_[:, k, o * 128:(o + 1) * 128],
                                               XNF[:, k, t_step * SEG:(t_step + 1) * SEG],
                                               start=(k == 0), stop=False)
                            for k in range(KC):
                                if k == KC - 1:
                                    E.tfence(t)
                                ins = t.matmul(bank[:, :], uT_[:, k, o * 128:(o + 1) * 128],
                                               cbf[:, k, :], start=False,
                                               stop=(k == KC - 1))
                            return ins
                        f_gate.prefetch = True
                        if o >= 1 and prev_evac.get(slot):
                            E.wait("T", "S", prev_evac[slot])
                        tm = E.step("T", f_gate, inc="T")
                        E.wait("S", "T", tm)
                        if o >= 1 and gm is not None:
                            E.wait("S", "G", gm)
                        sm = E.step("S", lambda sc, bank=bank, vb=vb, o=o, af=af,
                                    slot=slot: sc.activation(
                            FIO[:, slot, :], bank[:, :], af,
                            bias=VC[:, vb + o:vb + o + 1]), inc="S")
                        prev_evac[slot] = sm
                    E.wait("G", "S", sm)

                    def f_cu(g, o=o):
                        g.tensor_mul(FIO[:, 3, :], FIO[:, 0, :], c_sb[:, o, :])
                        g.tensor_mul(FIO[:, 0, :], FIO[:, 1, :], FIO[:, 2, :])
                        return g.tensor_add(c_sb[:, o, :], FIO[:, 3, :], FIO[:, 0, :])
                    gm = E.step("G", f_cu, inc="G")

            def f_ccf(sc):
                ins = None
                for k in range(KC):
                    ins = sc.copy(cbf[:, k, :], c_sb[:, k, :])
                return ins
            E.wait("S", "G", gm)
            gm_ccf = E.step("S", f_ccf, inc="S")

            t_gates_done = E.step("T", lambda t: t.nop(), inc="T")
            E.wait("Y", "T", t_gates_done)
            yd(lambda y: y.dma_start(
                out=woT, in_=wT_all.ap()[6 * 128:7 * 128, :].rearrange(
                    "p (a b) -> p a b", a=KC)))
            hw_wo = yd(lambda y: y.dma_start(
                out=uoT, in_=wT_all.ap()[7 * 128:8 * 128, :].rearrange(
                    "p (a b) -> p a b", a=KC)))

            E.wait("T", "S", gm_ccf)
            E.wait("T", "hw", hw_wo)
            E.wait("T", "ld", ld_brow3)
            for tc in range(4):
                for jg in range(2):
                    bank = pB[jg]

                    def f_zu(t, tc=tc, jg=jg, bank=bank):
                        ins = None
                        for k in range(KC):
                            ins = t.matmul(bank[:, :], cbf[:, k, tc * 128:(tc + 1) * 128],
                                           uoT[:, k, jg * 512:(jg + 1) * 512],
                                           start=(k == 0), stop=False)
                        E.tfence(t)
                        return t.matmul(bank[:, :], ONESB[:, :],
                                        BROW[:, jg * 512:(jg + 1) * 512],
                                        start=False, stop=True)
                    f_zu.prefetch = True
                    tm = E.step("T", f_zu, inc="T")
                    E.wait("V", "T", tm)
                    vm = E.step("V", lambda v, tc=tc, jg=jg, bank=bank: v.tensor_copy(
                        zu_v[:, tc, jg * 512:(jg + 1) * 512], bank[:, :]), inc="V")
                    E.wait("T", "V", vm)

            last_vo = None
            last_st = None
            for t_step in range(NSEG):
                vm_gb = build_bcast(t_step, 5 * D, 1)
                for tc in range(4):
                    r0 = t_step * SEG + tc * 128
                    if last_vo:
                        E.wait("Y", "V", last_vo)
                    hw_xn = yd(lambda y, r0=r0: y.dma_start(
                        out=FPA[:, 0, :], in_=xnew_dram[r0:r0 + 128, :]))
                    for jg in range(2):
                        bank = pA[jg]

                        def f_wo(t, t_step=t_step, tc=tc, jg=jg, bank=bank):
                            ins = None
                            for k in range(KC):
                                if k == KC - 1:
                                    E.tfence(t)
                                ins = t.matmul(bank[:, :],
                                               XNF[:, k, t_step * SEG + tc * 128:
                                                   t_step * SEG + (tc + 1) * 128],
                                               woT[:, k, jg * 512:(jg + 1) * 512],
                                               start=(k == 0), stop=(k == KC - 1))
                            return ins
                        f_wo.prefetch = True
                        tm = E.step("T", f_wo, inc="T")
                    E.wait("V", "T", tm)

                    def f_s1(v, tc=tc):
                        for jg in range(2):
                            v.tensor_add(FPB[:, 0, jg * 512:(jg + 1) * 512],
                                         pA[jg][:, :],
                                         zu_v[:, tc, jg * 512:(jg + 1) * 512])
                        return v.engine_nop()
                    vm_s1 = E.step("V", f_s1, inc="V")
                    E.wait("T", "V", vm_s1)
                    E.wait("S", "V", vm_s1)
                    if last_st:
                        E.wait("S", "st", last_st)
                    sm_th = E.step("S", lambda sc: sc.activation(
                        FPB[:, 1, :], FPB[:, 0, :], AF.Tanh), inc="S")
                    E.wait("V", "S", sm_th)
                    E.wait("V", "hw", hw_xn)
                    if last_st:
                        E.wait("V", "st", last_st)

                    def f_out(v):
                        v.tensor_mul(FPB[:, 0, :], FPB[:, 1, :], BC[:, 1, :])
                        return v.tensor_add(FPB[:, 1, :], FPB[:, 0, :],
                                            FPA[:, 0, :])
                    vm_o = E.step("V", f_out, inc="V")
                    last_vo = vm_o
                    E.wait("Y", "V", vm_o)
                    st = E.step("Y", lambda y, r0=r0: y.dma_start(
                        out=out_hbm[r0:r0 + 128, :], in_=FPB[:, 1, :]), inc="st")
                    last_st = st

        E.wait("Y", "st", E.c["st"])
        E.wait("G", "st", E.c["st"])

        chain_sems = {e: ctx.enter_context(nc.semaphore("chain_" + e))
                      for e in ("V", "S", "G")}
        E.sems["V"] = chain_sems["V"]
        E.sems["S"] = chain_sems["S"]
        E.sems["G"] = chain_sems["G"]
        E.prepass()
        with nc.Block() as block:
            @block.tensor
            def _(eng):
                E.run_queue("T", eng)

            @block.vector
            def _(eng):
                E.run_queue("V", Chained(eng, chain_sems["V"]))

            @block.scalar
            def _(eng):
                E.run_queue("S", Chained(eng, chain_sems["S"]))

            @block.gpsimd
            def _(eng):
                E.run_queue("G", Chained(eng, chain_sems["G"]))

            @block.sync
            def _(eng):
                E.run_queue("Y", eng)

    return nc


def _cols(vec, n):
    return np.ascontiguousarray(
        np.asarray(vec, dtype=np.float32).reshape(n, 128).T)


def prepare_inputs(inputs):
    x = np.asarray(inputs["L0_attn_x"], dtype=np.float32)
    vcols = np.concatenate([
        _cols(inputs["L0_attn_norm1_w"], 8),
        _cols(inputs["L0_attn_norm1_b"], 8),
        _cols(inputs["L0_attn_norm2_w"], 8),
        _cols(inputs["L0_attn_norm2_b"], 8),
        _cols(inputs["L0_attn_gate_b"], 16),
        _cols(inputs["L0_attn_state_b"], 8),
        _cols(inputs["L0_attn_mlp_b1"], 32),
        _cols(inputs["L0_ffn_in_c"], 8),
        _cols(inputs["L0_ffn_bf"], 8),
        _cols(inputs["L0_ffn_bi"], 8),
        _cols(inputs["L0_ffn_bc"], 8),
    ], axis=1)
    rows3 = np.stack([
        np.asarray(inputs["L0_attn_out_b"], dtype=np.float32),
        np.asarray(inputs["L0_attn_mlp_b2"], dtype=np.float32),
        np.asarray(inputs["L0_ffn_bo"], dtype=np.float32),
    ], axis=0)
    ident = np.eye(128, dtype=ml_dtypes.bfloat16)
    ffn_ws = [np.ascontiguousarray(np.asarray(inputs[k], dtype=np.float32))
              for k in ("L0_ffn_wf", "L0_ffn_uf", "L0_ffn_wi", "L0_ffn_ui",
                        "L0_ffn_wc", "L0_ffn_uc", "L0_ffn_wo", "L0_ffn_uo")]
    shared = dict(
        gate_w=np.ascontiguousarray(np.asarray(inputs["L0_attn_gate_w"], dtype=np.float32)),
        state_w=np.ascontiguousarray(np.asarray(inputs["L0_attn_state_w"], dtype=np.float32)),
        out_w=np.ascontiguousarray(np.asarray(inputs["L0_attn_out_w"], dtype=np.float32)),
        mlp_w1=np.ascontiguousarray(np.asarray(inputs["L0_attn_mlp_w1"], dtype=np.float32)),
        mlp_w2=np.ascontiguousarray(np.asarray(inputs["L0_attn_mlp_w2"], dtype=np.float32)),
        wmod=np.ascontiguousarray(np.asarray(inputs["L0_wmod"], dtype=np.float32)),
        cond_c=np.ascontiguousarray(np.asarray(inputs["L0_c"], dtype=np.float32)),
        vcols=vcols, rows3=rows3, ident=ident,
    )
    in_maps = []
    for c in range(NCORE):
        m = dict(shared)
        m["x"] = np.ascontiguousarray(
            x[:, SEG * c:SEG * (c + 1), :].reshape(TOK, D))
        m["my_ffn_w"] = ffn_ws[c]
        mask = (np.arange(8) < c).astype(np.float32)
        m["mrow"] = mask.reshape(1, 8).copy()
        m["invmrow"] = (1.0 - mask).reshape(1, 8).copy()
        m["mcols"] = np.ascontiguousarray(
            np.broadcast_to(mask, (128, 8)).astype(np.float32))
        in_maps.append(m)
    return in_maps


def kernel(**inputs):
    in_maps = prepare_inputs(inputs)
    nc = build_program()
    res = run_bass_kernel_spmd(nc, in_maps, core_ids=list(range(NCORE)))
    out = np.empty((NSEG, NCORE * SEG, D), dtype=np.float32)
    for c in range(NCORE):
        r = np.asarray(res.results[c]["out"]).reshape(NSEG, SEG, D)
        out[:, SEG * c:SEG * (c + 1), :] = r
    return out
